# version 08
# baseline (speedup 1.0000x reference)
import numpy as np
import ml_dtypes
from contextlib import ExitStack

import concourse.bass as bass
import concourse.mybir as mybir
from concourse.bass_utils import run_bass_kernel_spmd

F32 = mybir.dt.float32
BF16 = mybir.dt.bfloat16
U8 = mybir.dt.uint8
AF = mybir.ActivationFunctionType
ALU = mybir.AluOpType

D = 4096
DC = 32
B = 4
T = 2048
TOKALL = B * T
TOK = 1024
H = 32
HD = 128
NXT = 16
XTW = 256
KT = 16
QTS = 8

EXP_BIAS = -12.0
EXP_SCALE = 1.0 / float(np.sqrt(128.0))
EPS = 1e-5
MASK_NEG = -1.0e30
LORA_SCALE = 2.0

ARENA_BYTES = 210944

OFF_XT = (0, 16384)
OFF_MASKS = 0
OFF_EXPS = (4096, 6144, 8192, 10240, 12288)
OFF_QH = (14336, 16384)
OFF_LATB = 0
OFF_LBTB = 8192
OFF_HB = 16384
OFF_R1RD = (18432, 22528, 90112, 94208)
OFF_Y32 = 26624
OFF_R2RD = (0, 4096, 8192, 12288)
OFF_OUTB = (16384, 20480)

OFF_WKV = 32768
OFF_WBLK = (32768, 40960)
OFF_QOUT = (49152, 51200)
OFF_XM32 = (49152, 53248)
OFF_R2BUF = (49152, 53248)
OFF_R1BUF = (57344, 61440)

OFF_KT = 65536
OFF_SV = tuple(65536 + i * 4096 for i in range(6))
OFF_LINV32 = 65536
OFF_LTMP = (196928, 201024)
OFF_V = 98304
OFF_SOFTB = (98304, 102400, 106496, 110592)
OFF_BCM1 = 98304
OFF_BCA1 = 102400
OFF_BCM2 = 106496
OFF_BCA2 = 110592
OFF_LBT32 = 114688

OFF_XMBF = 131072
OFF_CTX = 131072
OFF_YBF = 131072

OFF_CONSTS = 196608
OFF_ONESB = 196640
OFF_BO = 196672
OFF_BI = 196800
OFF_LALL = 196928
OFF_SVEC = 201024
OFF_R1BF = 205120
OFF_SQBF = 207168
OFF_IDEN = 209216

_CACHED = {}
PHASES = 6


def build_nc():
    nc = bass.Bass("TRN2", target_bir_lowering=False)

    d_xs = nc.dram_tensor("xs", [NXT, 128, DC, XTW], F32, kind="ExternalInput").ap()
    d_cin = nc.dram_tensor("cin", [NXT, 128, 1024], BF16, kind="Internal").ap()
    d_cout_a = nc.dram_tensor("cout_a", [2, 8, 128, 1024], BF16, kind="Internal").ap()
    d_cout_b = nc.dram_tensor("cout_b", [2, 8, 128, 1024], BF16, kind="Internal").ap()
    d_xm = nc.dram_tensor("xm", [128, DC, TOK], F32, kind="ExternalInput").ap()
    d_wq = nc.dram_tensor("wq", [DC, 128, DC, 128], F32, kind="ExternalInput").ap()
    d_wo = nc.dram_tensor("wo", [DC, 128, DC, 128], F32, kind="ExternalInput").ap()
    d_wt = nc.dram_tensor("wt", [DC, 128, DC, 128], F32, kind="ExternalInput").ap()
    d_wkv = nc.dram_tensor("wkv", [128, DC, 512], F32, kind="ExternalInput").ap()
    d_lat = nc.dram_tensor("lat", [128, DC, 128], F32, kind="ExternalInput").ap()
    d_lbt = nc.dram_tensor("lbt", [128, D], F32, kind="ExternalInput").ap()
    d_lw = nc.dram_tensor("lw", [128, 1], F32, kind="ExternalInput").ap()
    d_bo = nc.dram_tensor("bo", [128, DC], F32, kind="ExternalInput").ap()
    d_bi = nc.dram_tensor("bi", [128, DC], F32, kind="ExternalInput").ap()
    d_masks = nc.dram_tensor("masks", [128, KT, 128], F32, kind="ExternalInput").ap()
    d_iden = nc.dram_tensor("iden", [128, 128], F32, kind="ExternalInput").ap()

    d_out = nc.dram_tensor("out", [DC, 128, TOK], F32, kind="ExternalOutput").ap()

    d_qsc = nc.dram_tensor("qsc", [H, 128, TOK], BF16, kind="Internal").ap()
    d_r1 = nc.dram_tensor("r1sc", [DC, 128, TOK], F32, kind="Internal").ap()
    d_r2 = nc.dram_tensor("r2sc", [DC, 128, TOK], F32, kind="Internal").ap()
    d_linv = nc.dram_tensor("linv", [H, TOK], F32, kind="Internal").ap()
    d_vec4 = nc.dram_tensor("vec4", [4, TOK], F32, kind="Internal").ap()

    arena = nc.alloc_sbuf_tensor("arena", [128, ARENA_BYTES], U8).ap()

    def vbf(off, cols, parts=128):
        return arena[0:parts, off:off + 2 * cols].bitcast(BF16)

    def vf32(off, cols, parts=128):
        return arena[0:parts, off:off + 4 * cols].bitcast(F32)

    sb_consts = vf32(OFF_CONSTS, 4)
    sb_ones = vbf(OFF_ONESB, 1)
    sb_bo = vf32(OFF_BO, DC)
    sb_bi = vf32(OFF_BI, DC)
    sb_linv32 = arena[0:32, OFF_LINV32:OFF_LINV32 + 4 * TOK].bitcast(F32)

    def sv(i):
        return arena[0:1, OFF_SV[i]:OFF_SV[i] + 4 * TOK].bitcast(F32)

    def ytmp(i):
        return vf32(OFF_SV[i % 6], TOK)

    def ltmp(i):
        return arena[0:1, OFF_LTMP[i]:OFF_LTMP[i] + 4 * TOK].bitcast(F32)
    sb_r1bf = vbf(OFF_R1BF, TOK)
    sb_sqbf = vbf(OFF_SQBF, TOK)
    sb_wkv = vbf(OFF_WKV, DC * 512)
    sb_kt = vbf(OFF_KT, 2 * B * T // 1)
    sb_v = vbf(OFF_V, 16384)
    sb_masks = vbf(OFF_MASKS, KT * 128)
    sb_iden = vbf(OFF_IDEN, 128)
    sb_xmbf = vbf(OFF_XMBF, DC * TOK)
    sb_ctx = vbf(OFF_CTX, H * TOK)
    sb_ybf = vbf(OFF_YBF, DC * TOK)
    sb_latb = vbf(OFF_LATB, DC * 128)
    sb_lbtb = vbf(OFF_LBTB, D)
    sb_lbt32 = vf32(OFF_LBT32, D)
    sb_hb = vbf(OFF_HB, TOK)
    sb_y32 = vf32(OFF_Y32, TOK)
    sb_bcm1 = vf32(OFF_BCM1, TOK)
    sb_bca1 = vf32(OFF_BCA1, TOK)
    sb_bcm2 = vf32(OFF_BCM2, TOK)
    sb_bca2 = vf32(OFF_BCA2, TOK)

    prog = {"g": [], "s": [], "t": [], "v": [], "c": []}
    cur_phase = [0]
    dma_fin = {}

    def em(e, f):
        prog[e].append((cur_phase[0], f))

    VSEQ = [0]

    def vsync(v, inst):
        VSEQ[0] += 1
        inst.then_inc(SM["v_seq"], 1)
        v.wait_ge(SM["v_seq"], VSEQ[0])

    def note_dma(semname, inc=16):
        key = semname
        dma_fin[key] = dma_fin.get(key, 0) + inc
        dma_fin.setdefault("_byphase", {}).setdefault(cur_phase[0], {})
        ph = dma_fin["_byphase"][cur_phase[0]]
        ph[key] = ph.get(key, 0) + inc

    with ExitStack() as es:
        P = [es.enter_context(nc.psum_tensor(f"P{i}", [128, 1024], F32)) for i in range(4)]
        P01, P23, P45, P67 = [p[:] for p in P]

        semnames = [
            "gx0", "gx1", "gw0", "gw1", "s_wkv", "s_xmbf", "s_lat", "s_lbt",
            "s_ini", "s_masks", "s_linv", "s_vec4", "s_bc",
            "ss_sm0", "ss_sm1", "ss_sm2", "ss_sm3", "sq_qh0", "sq_qh1", "ss_qout0", "ss_qout1",
            "ss_lt0", "ss_lt1", "s_linvld", "sc_st0", "sc_st1", "s_coll", "s_kvrb",
            "sx_xm0", "sx_xm1", "ss_r10", "ss_r11", "sr1rd0", "sr1rd1", "sr1rd2", "sr1rd3",
            "gr1rd0", "gr1rd1", "gr1rd2", "gr1rd3",
            "ss_r20", "ss_r21", "sr2rd0", "sr2rd1", "sr2rd2", "sr2rd3", "so_out0", "so_out1",
            "v_ini", "t_ph1", "v_ev1", "t_mmq", "v_evq", "t_mms", "v_msk",
            "s_exp", "t_mmpc", "v_evl", "v_evctx", "v_norm", "v_nmul",
            "t_mmwo", "v_r1", "s_stats", "t_stats", "v_stat", "s_sqrt",
            "v_y", "t_mmh", "v_evh", "v_lbt", "t_mmf", "v_r2", "s_st2",
            "t_st2", "v_stat2", "s_sqrt2", "v_out", "v_seq", "v_ysub", "v_osub",
        ]
        SM = {n: es.enter_context(nc.semaphore(n)) for n in semnames}

        def f_v_init(v):
            v.memset(sb_consts[:, 0:1], EXP_BIAS)
            v.memset(sb_consts[:, 1:2], EXP_SCALE)
            v.memset(sb_ones[:], 1.0).then_inc(SM["v_ini"], 1)
        em("v", f_v_init)

        sb_lw = arena[0:128, OFF_CONSTS + 8:OFF_CONSTS + 12].bitcast(F32)

        def f_s_init(sy):
            sy.dma_start(out=sb_bo[:], in_=d_bo[:]).then_inc(SM["s_ini"], 16)
            sy.dma_start(out=sb_bi[:], in_=d_bi[:]).then_inc(SM["s_ini"], 16)
            sy.dma_start(out=sb_lw[:], in_=d_lw[:]).then_inc(SM["s_ini"], 16)
        em("s", f_s_init)
        S_INI_TOTAL = 48

        cur_phase[0] = 1
        gxc = [0, 0]

        def emit_xt_dma(i):
            slot = i % 2
            gxc[slot] += 1
            tgt = 16 * gxc[slot]
            def f(g):
                if i >= 2:
                    g.wait_ge(SM["t_ph1"], 4 * (i - 1))
                out = vbf(OFF_XT[slot], DC * XTW).rearrange("p (c t) -> p c t", c=DC)
                g.dma_start(out=out, in_=d_xs[i]).then_inc(SM["gx%d" % slot], 16)
            em("g", f)
            return tgt

        def f_iden(g):
            g.dma_start(out=sb_iden[:], in_=d_iden[:]).then_inc(SM["s_masks"], 16)
        em("g", f_iden)

        xt_tgt = [None] * NXT
        xt_tgt[0] = emit_xt_dma(0)

        def f_wkv(g):
            out = sb_wkv.rearrange("p (c w) -> p c w", c=DC)
            g.dma_start(out=out, in_=d_wkv[:]).then_inc(SM["s_wkv"], 16)
        em("g", f_wkv)
        def f_xmbf(g, q):
            out = sb_xmbf.rearrange("p (c t) -> p c t", c=DC)
            g.dma_start(out=out[:, q * 8:(q + 1) * 8, :],
                        in_=d_xm[:, q * 8:(q + 1) * 8, :]).then_inc(SM["s_xmbf"], 16)
        nxt_i = 1
        for q in range(4):
            while nxt_i < 5 + q * 3:
                xt_tgt[nxt_i] = emit_xt_dma(nxt_i)
                nxt_i += 1
            em("g", lambda g, q=q: f_xmbf(g, q))
        while nxt_i < NXT:
            xt_tgt[nxt_i] = emit_xt_dma(nxt_i)
            nxt_i += 1

        def emit_ph1_tensor():
            for i in range(NXT):
                slot = i % 2
                tgt = xt_tgt[i]
                def f(t, i=i, slot=slot, tgt=tgt):
                    t.wait_ge(SM["gx%d" % slot], tgt)
                    if i == 0:
                        t.wait_ge(SM["s_wkv"], 16)
                    xt = vbf(OFF_XT[slot], DC * XTW)
                    for gl in range(2):
                        pair = P[gl][:]
                        if i >= 1:
                            t.wait_ge(SM["v_ev1"], 4 * (i - 1) + gl + 1)
                        for kc in range(DC):
                            mm = t.matmul(pair[:, 0:XTW],
                                          sb_wkv[:, kc * 512 + gl * 128: kc * 512 + gl * 128 + 128],
                                          xt[:, kc * XTW:(kc + 1) * XTW],
                                          start=(kc == 0), stop=(kc == DC - 1))
                            if kc == DC - 1:
                                mm.then_inc(SM["t_ph1"], 1)
                    for s in range(2):
                        pair = P[2 + s][:]
                        if i >= 1:
                            t.wait_ge(SM["v_ev1"], 4 * (i - 1) + 2 + s + 1)
                        for kc in range(DC):
                            mm = t.matmul(pair[:, 0:256],
                                          xt[:, kc * XTW + s * 128: kc * XTW + s * 128 + 128],
                                          sb_wkv[:, kc * 512 + 256: kc * 512 + 512],
                                          start=(kc == 0), stop=(kc == DC - 1))
                            if kc == DC - 1:
                                mm.then_inc(SM["t_ph1"], 1)
                em("t", f)
        emit_ph1_tensor()

        STAG = (196928, 198976)
        def emit_ph1_vector():
            for i in range(NXT):
                def f(v, i=i):
                    stag = vbf(STAG[i % 2], 1024)
                    if i >= 2:
                        v.wait_ge(SM["sc_st%d" % (i % 2)], 16 * (i // 2))
                    for gl in range(2):
                        v.wait_ge(SM["t_ph1"], 4 * i + gl + 1)
                        v.tensor_copy(stag[:, gl * XTW:(gl + 1) * XTW],
                                      P[gl][:][:, 0:XTW]).then_inc(SM["v_ev1"], 1)
                    for s in range(2):
                        v.wait_ge(SM["t_ph1"], 4 * i + 2 + s + 1)
                        v.tensor_copy(stag[:, 512 + s * 256: 512 + (s + 1) * 256],
                                      P[2 + s][:][:, 0:256]).then_inc(SM["v_ev1"], 1)
                em("v", f)

                def f_s(sy, i=i):
                    sy.wait_ge(SM["v_ev1"], 4 * (i + 1))
                    sy.dma_start(out=d_cin[i], in_=vbf(STAG[i % 2], 1024)).then_inc(SM["sc_st%d" % (i % 2)], 16)
                em("s", f_s)
        emit_ph1_vector()
        T_PH1_TOTAL = 4 * NXT
        V_EV1_TOTAL = 4 * NXT

        RG = [[0, 1], [2, 3], [4, 5], [6, 7]]

        def f_coll_a(g):
            g.wait_ge(SM["sc_st0"], 16 * 4)
            g.wait_ge(SM["sc_st1"], 16 * 4)
            g.collective_compute("AllGather", mybir.AluOpType.bypass,
                                 ins=[d_cin[0:8]], outs=[d_cout_a[:]],
                                 replica_groups=RG).then_inc(SM["s_coll"], 1)
        em("g", f_coll_a)

        def f_coll_b(g):
            g.wait_ge(SM["sc_st0"], 16 * 8)
            g.wait_ge(SM["sc_st1"], 16 * 8)
            g.collective_compute("AllGather", mybir.AluOpType.bypass,
                                 ins=[d_cin[8:16]], outs=[d_cout_b[:]],
                                 replica_groups=RG).then_inc(SM["s_coll"], 1)

        def emit_rb(which, coll_n):
            cout = d_cout_a if which == 0 else d_cout_b

            def f(sy, which=which, coll_n=coll_n, cout=cout):
                sy.wait_ge(SM["s_coll"], coll_n)
                for half in range(2):
                    bk = which + 2 * half
                    for gl in range(2):
                        kdst = sb_kt[:, gl * 8192 + bk * 2048: gl * 8192 + (bk + 1) * 2048]
                        sy.dma_start(
                            out=kdst.rearrange("p (t c) -> p t c", t=8),
                            in_=cout[half][:, :, gl * 256:(gl + 1) * 256].rearrange(
                                "t p c -> p t c")).then_inc(SM["s_kvrb"], 16)
                        vdst = sb_v[:, gl * 8192 + bk * 2048: gl * 8192 + (bk + 1) * 2048]
                        for s in range(2):
                            sy.dma_start(
                                out=vdst.rearrange("p (t s c) -> p t s c", t=8, s=2)[:, :, s, :],
                                in_=cout[half][:, :, 512 + s * 256 + gl * 128:
                                               512 + s * 256 + (gl + 1) * 128].rearrange(
                                    "t p c -> p t c")).then_inc(SM["s_kvrb"], 16)
            em("s", f)

        emit_rb(0, 1)

        cur_phase[0] = 2
        gwc = [0, 0]

        def emit_wblk_dma(src_ap, mc, waits):
            slot = mc % 2
            gwc[slot] += 1
            tgt = 16 * gwc[slot]
            def f(g, waits=tuple(waits)):
                for sn, val in waits:
                    g.wait_ge(SM[sn], val)
                out = vbf(OFF_WBLK[slot], DC * 128).rearrange("p (k q) -> p k q", k=DC)
                g.dma_start(out=out, in_=src_ap).then_inc(SM["gw%d" % slot], 16)
            em("g", f)
            return tgt

        wq_tgt = []
        for mc in range(DC):
            waits = [("t_ph1", T_PH1_TOTAL)] if mc < 2 else [("t_mmq", mc - 1)]
            wq_tgt.append(emit_wblk_dma(d_wq[mc], mc, waits))
            if mc == 1:
                cur_phase[0] = 1
                em("g", f_coll_b)
                cur_phase[0] = 2

        qoutc = [0, 0]
        for mc in range(DC):
            slot = mc % 2

            def f_t(t, mc=mc, slot=slot):
                t.wait_ge(SM["gw%d" % slot], wq_tgt[mc])
                if mc == 0:
                    t.wait_ge(SM["s_xmbf"], 64)
                if mc < 2:
                    t.wait_ge(SM["v_ev1"], 4 * NXT)
                if mc >= 2:
                    t.wait_ge(SM["v_evq"], mc - 1)
                pair = P[mc % 2][:]
                wb = vbf(OFF_WBLK[slot], DC * 128)
                for kc in range(DC):
                    st = wb[:, kc * 128:(kc + 1) * 128]
                    t.matmul(pair[:, 0:512], st, sb_xmbf[:, kc * TOK: kc * TOK + 512],
                             start=(kc == 0), stop=(kc == DC - 1))
                    if kc == DC - 1:
                        t.matmul(pair[:, 512:1024], st, sb_xmbf[:, kc * TOK + 512:(kc + 1) * TOK],
                                 start=False, stop=True).then_inc(SM["t_mmq"], 1)
                    else:
                        t.matmul(pair[:, 512:1024], st, sb_xmbf[:, kc * TOK + 512:(kc + 1) * TOK],
                                 start=(kc == 0), stop=False)
            em("t", f_t)

            def f_v(v, mc=mc, slot=slot):
                v.wait_ge(SM["t_mmq"], mc + 1)
                if mc >= 2:
                    v.wait_ge(SM["ss_qout%d" % slot], 16 * (mc // 2))
                v.tensor_copy(vbf(OFF_QOUT[slot], TOK), P[mc % 2][:]).then_inc(SM["v_evq"], 1)
            em("v", f_v)

            qoutc[slot] += 1

            def f_s(sy, mc=mc, slot=slot):
                sy.wait_ge(SM["v_evq"], mc + 1)
                sy.dma_start(out=d_qsc[mc], in_=vbf(OFF_QOUT[slot], TOK)).then_inc(SM["ss_qout%d" % slot], 16)
            em("s", f_s)
            if mc == 4:
                cur_phase[0] = 1
                emit_rb(1, 2)
                cur_phase[0] = 2
        T_MMQ_TOTAL = DC

        cur_phase[0] = 3
        def f_masks(g):
            g.wait_ge(SM["t_ph1"], T_PH1_TOTAL)
            out = sb_masks.rearrange("p (k q) -> p k q", k=KT)
            g.dma_start(out=out, in_=d_masks[:]).then_inc(SM["s_masks"], 16)
        em("g", f_masks)

        qh_tgt = [16 * (h // 2 + 1) for h in range(H)]

        def emit_qh(h):
            slot = h % 2

            def f(sy, h=h, slot=slot):
                if h < 2:
                    sy.wait_ge(SM["t_ph1"], T_PH1_TOTAL)
                    sy.wait_ge(SM["ss_qout0"], 16 * (DC // 2))
                    sy.wait_ge(SM["ss_qout1"], 16 * (DC // 2))
                else:
                    sy.wait_ge(SM["t_mms"], KT * (h - 1))
                sy.dma_start(out=vbf(OFF_QH[slot], TOK), in_=d_qsc[h]).then_inc(SM["sq_qh%d" % slot], 16)
            em("s", f)

        emit_qh(0)
        emit_qh(1)

        pending_lc = []

        def emit_S(h, kc):
            gl = h // 16
            bk = (h % 16) // 4
            kvbase = gl * 8192 + bk * 2048
            slot = h % 2

            def f_t(t, h=h, kc=kc, kvbase=kvbase, slot=slot):
                Kg = KT * h + kc
                s0 = (kc // 2) * 128
                pair = P[kc % 2][:]
                if h == 0 and kc == 0:
                    t.wait_ge(SM["s_kvrb"], 16 * 24)
                    t.wait_ge(SM["v_ini"], 1)
                    t.wait_ge(SM["s_masks"], 32)
                if kc == 0:
                    t.wait_ge(SM["sq_qh%d" % slot], qh_tgt[h])
                    if h < 2:
                        t.wait_ge(SM["ss_qout%d" % (h % 2)], 16 * ((h // 2) + 1))
                if Kg >= 2:
                    t.wait_ge(SM["s_exp"], Kg - 1)
                qh = vbf(OFF_QH[slot], TOK)
                st = sb_kt[:, kvbase + kc * 128: kvbase + (kc + 1) * 128]
                mk = sb_masks[:, kc * 128:(kc + 1) * 128]
                if s0 < 512:
                    t.matmul(pair[:, s0:512], st, qh[:, s0:512], start=True, stop=False)
                    t.matmul(pair[:, 512:1024], st, qh[:, 512:1024], start=True, stop=True)
                    t.matmul(pair[:, s0:s0 + 128], mk, sb_iden[:], start=False, stop=True).then_inc(SM["t_mms"], 1)
                else:
                    t.matmul(pair[:, s0:1024], st, qh[:, s0:1024], start=True, stop=False)
                    t.matmul(pair[:, s0:s0 + 128], mk, sb_iden[:], start=False, stop=True).then_inc(SM["t_mms"], 1)
            em("t", f_t)

        def emit_LC(h, kc):
            gl = h // 16
            bk = (h % 16) // 4
            kvbase = gl * 8192 + bk * 2048

            def f_t(t, h=h, kc=kc, kvbase=kvbase):
                Kg = KT * h + kc
                s0 = (kc // 2) * 128
                t.wait_ge(SM["s_exp"], Kg + 1)
                if kc == 0:
                    t.wait_ge(SM["v_evl"], h)
                    t.wait_ge(SM["v_evctx"], h)
                ex = vbf(OFF_EXPS[Kg % 5], TOK)
                if s0 < 512:
                    t.matmul(P67[0:1, s0:512], sb_ones[:], ex[:, s0:512],
                             start=(kc == 0), stop=(kc == 7))
                    t.matmul(P67[0:1, 512:1024], sb_ones[:], ex[:, 512:1024],
                             start=(kc == 0), stop=(kc == KT - 1))
                else:
                    t.matmul(P67[0:1, s0:1024], sb_ones[:], ex[:, s0:1024],
                             start=False, stop=(kc == KT - 1))
                stv = sb_v[:, kvbase + kc * 128: kvbase + (kc + 1) * 128]
                if s0 < 512:
                    t.matmul(P45[:, s0:512], stv, ex[:, s0:512],
                             start=(kc == 0), stop=(kc == 7))
                    t.matmul(P45[:, 512:1024], stv, ex[:, 512:1024],
                             start=(kc == 0), stop=(kc == KT - 1)).then_inc(SM["t_mmpc"], 1)
                else:
                    t.matmul(P45[:, s0:1024], stv, ex[:, s0:1024],
                             start=False, stop=(kc == KT - 1)).then_inc(SM["t_mmpc"], 1)
            em("t", f_t)

        for h in range(H):
            for kc in range(KT):
                emit_S(h, kc)
                pending_lc.append((h, kc))
                if len(pending_lc) > 2:
                    emit_LC(*pending_lc.pop(0))

            def f_v(v, h=h):
                v.wait_ge(SM["t_mmpc"], KT * (h + 1))
                if h >= 2:
                    v.wait_ge(SM["ss_lt%d" % (h % 2)], 16 * (h // 2))
                v.tensor_copy(ltmp(h % 2), P67[0:1, 0:1024]).then_inc(SM["v_evl"], 1)
                v.tensor_copy(sb_ctx[:, h * TOK:(h + 1) * TOK], P45[:]).then_inc(SM["v_evctx"], 1)
            em("v", f_v)

            def f_c(sc, h=h):
                for kc in range(KT):
                    Kg = KT * h + kc
                    s0 = (kc // 2) * 128
                    sc.wait_ge(SM["t_mms"], Kg + 1)
                    if Kg == 0:
                        sc.wait_ge(SM["v_ini"], 1)
                    if Kg >= 5:
                        sc.wait_ge(SM["t_mmpc"], Kg - 4)
                    pair = P[kc % 2][:]
                    sc.activation(vbf(OFF_EXPS[Kg % 5], TOK)[:, s0:1024], pair[:, s0:1024],
                                  AF.Exp, bias=sb_consts[:, 0:1], scale=sb_consts[:, 1:2]).then_inc(SM["s_exp"], 1)
            em("c", f_c)

            def f_s_lt(sy, h=h):
                sy.wait_ge(SM["v_evl"], h + 1)
                sy.dma_start(out=d_linv[h:h + 1, :], in_=ltmp(h % 2)).then_inc(SM["ss_lt%d" % (h % 2)], 16)
            em("s", f_s_lt)
            if h + 2 < H:
                emit_qh(h + 2)
        while pending_lc:
            emit_LC(*pending_lc.pop(0))

        T_MMS_TOTAL = KT * H
        T_MMPC_TOTAL = KT * H

        def f_s_linvld(sy):
            sy.wait_ge(SM["ss_lt0"], 16 * (H // 2))
            sy.wait_ge(SM["ss_lt1"], 16 * (H // 2))
            sy.wait_ge(SM["t_mmpc"], T_MMPC_TOTAL)
            sy.dma_start(out=sb_linv32[:, :], in_=d_linv[:]).then_inc(SM["s_linvld"], 16)
        em("s", f_s_linvld)

        def f_v_norm(v):
            v.wait_ge(SM["s_linvld"], 16)
            v.reciprocal(sb_linv32[:, :], sb_linv32[:, :]).then_inc(SM["v_norm"], 1)
        em("v", f_v_norm)

        def f_s_norm(sy):
            sy.wait_ge(SM["v_norm"], 1)
            sy.dma_start(out=d_linv[:], in_=sb_linv32[:, :]).then_inc(SM["s_linv"], 16)
        em("s", f_s_norm)

        for hh in range(H):
            slot = hh % 4
            tgt = 16 * (hh // 4 + 1)

            def f_s(sy, hh=hh, slot=slot):
                if hh == 0:
                    sy.wait_ge(SM["s_linv"], 16)
                if hh >= 4:
                    sy.wait_ge(SM["v_nmul"], hh - 3)
                sy.dma_start(out=vf32(OFF_SOFTB[slot], TOK),
                             in_=d_linv[hh:hh + 1, :].broadcast_to([128, TOK])).then_inc(SM["ss_sm%d" % slot], 16)
            em("s", f_s)

            def f_v(v, hh=hh, slot=slot, tgt=tgt):
                v.wait_ge(SM["ss_sm%d" % slot], tgt)
                sl = sb_ctx[:, hh * TOK:(hh + 1) * TOK]
                v.tensor_tensor(sl, sl, vf32(OFF_SOFTB[slot], TOK), ALU.mult).then_inc(SM["v_nmul"], 1)
            em("v", f_v)

        cur_phase[0] = 4
        r1rdc = {"s": [0, 0, 0, 0], "g": [0, 0, 0, 0]}

        def emit_r1rd(idx, extra_waits, eng="s"):
            slot = idx % 4
            r1rdc[eng][slot] += 1
            tgt = 16 * r1rdc[eng][slot]
            mc = idx % DC
            off = OFF_R1RD[slot]
            pfx = "sr1rd" if eng == "s" else "gr1rd"

            def f(sy, mc=mc, slot=slot, off=off, pfx=pfx, extra_waits=tuple(extra_waits)):
                for sn, val in extra_waits:
                    sy.wait_ge(SM[sn], val)
                sy.dma_start(out=vf32(off, TOK), in_=d_r1[mc]).then_inc(SM[pfx + "%d" % slot], 16)
            em(eng, f)
            return tgt

        r1rd_tgt = [None] * 64
        wo_tgt = []
        for mc in range(DC):
            waits = [("t_mmq", T_MMQ_TOTAL)] if mc < 2 else [("t_mmwo", mc - 1)]
            wo_tgt.append(emit_wblk_dma(d_wo[mc], mc, waits))

        xm_tgt = [16 * (mc // 2 + 1) for mc in range(DC)]

        def emit_xm(mc):
            slot = mc % 2

            def f_s(sy, mc=mc, slot=slot):
                if mc < 2:
                    sy.wait_ge(SM["v_evq"], DC)
                    sy.wait_ge(SM["ss_qout0"], 16 * (DC // 2))
                    sy.wait_ge(SM["ss_qout1"], 16 * (DC // 2))
                else:
                    sy.wait_ge(SM["v_r1"], mc - 1)
                sy.dma_start(out=vf32(OFF_XM32[slot], TOK), in_=d_xm[:, mc, :]).then_inc(SM["sx_xm%d" % slot], 16)
            em("s", f_s)

        emit_xm(0)
        emit_xm(1)

        r1c = [0, 0]
        pending_stats = None
        for mc in range(DC):
            slot = mc % 2

            def f_t(t, mc=mc, slot=slot):
                t.wait_ge(SM["gw%d" % slot], wo_tgt[mc])
                if mc >= 2:
                    t.wait_ge(SM["v_r1"], mc - 1)
                pair = P[mc % 2][:]
                wb = vbf(OFF_WBLK[slot], DC * 128)
                for kc in range(DC):
                    if mc == 0:
                        t.wait_ge(SM["v_nmul"], kc + 1)
                    st = wb[:, kc * 128:(kc + 1) * 128]
                    t.matmul(pair[:, 0:512], st, sb_ctx[:, kc * TOK: kc * TOK + 512],
                             start=(kc == 0), stop=(kc == DC - 1))
                    if kc == DC - 1:
                        t.matmul(pair[:, 512:1024], st, sb_ctx[:, kc * TOK + 512:(kc + 1) * TOK],
                                 start=False, stop=True).then_inc(SM["t_mmwo"], 1)
                    else:
                        t.matmul(pair[:, 512:1024], st, sb_ctx[:, kc * TOK + 512:(kc + 1) * TOK],
                                 start=(kc == 0), stop=False)
            em("t", f_t)
            if pending_stats is not None:
                em("t", pending_stats)

            def f_t_stats(t, mc=mc):
                t.wait_ge(SM["s_stats"], 2 * (mc + 1))
                if mc == 0:
                    t.wait_ge(SM["v_evl"], H)
                    t.wait_ge(SM["v_evctx"], H)
                    t.wait_ge(SM["v_nmul"], H)
                t.matmul(P45[0:1, 0:512], sb_ones[:], sb_r1bf[:, 0:512],
                         start=(mc == 0), stop=(mc == DC - 1))
                t.matmul(P45[0:1, 512:1024], sb_ones[:], sb_r1bf[:, 512:1024],
                         start=(mc == 0), stop=(mc == DC - 1))
                t.matmul(P67[0:1, 0:512], sb_ones[:], sb_sqbf[:, 0:512],
                         start=(mc == 0), stop=(mc == DC - 1))
                t.matmul(P67[0:1, 512:1024], sb_ones[:], sb_sqbf[:, 512:1024],
                         start=(mc == 0), stop=(mc == DC - 1)).then_inc(SM["t_stats"], 1)

            def f_v(v, mc=mc, slot=slot):
                v.wait_ge(SM["t_mmwo"], mc + 1)
                v.wait_ge(SM["sx_xm%d" % slot], xm_tgt[mc])
                if mc == 0:
                    v.wait_ge(SM["s_ini"], S_INI_TOTAL)
                if mc >= 2:
                    v.wait_ge(SM["ss_r1%d" % slot], 16 * (mc // 2))
                    v.wait_ge(SM["s_stats"], 2 * (mc - 1))
                pair = P[mc % 2][:]
                r1b = vf32(OFF_R1BUF[slot], TOK)
                xm32 = vf32(OFF_XM32[slot], TOK)
                v.scalar_tensor_tensor(r1b[:, 0:512], pair[:, 0:512], sb_bo[:, mc:mc + 1],
                                       xm32[:, 0:512], ALU.add, ALU.add)
                v.scalar_tensor_tensor(r1b[:, 512:1024], pair[:, 512:1024], sb_bo[:, mc:mc + 1],
                                       xm32[:, 512:1024], ALU.add, ALU.add).then_inc(SM["v_r1"], 1)
            em("v", f_v)

            def f_c(sc, mc=mc, slot=slot):
                sc.wait_ge(SM["v_r1"], mc + 1)
                if mc >= 1:
                    sc.wait_ge(SM["t_stats"], mc)
                r1b = vf32(OFF_R1BUF[slot], TOK)
                sc.copy(sb_r1bf[:], r1b[:])
                sc.activation(sb_sqbf[:], r1b[:], AF.Square, bias=0.0, scale=1.0).then_inc(SM["s_stats"], 2)
            em("c", f_c)
            pending_stats = f_t_stats

            r1c[slot] += 1

            def f_s2(sy, mc=mc, slot=slot):
                sy.wait_ge(SM["v_r1"], mc + 1)
                sy.dma_start(out=d_r1[mc], in_=vf32(OFF_R1BUF[slot], TOK)).then_inc(SM["ss_r1%d" % slot], 16)
            em("s", f_s2)
            if mc < 4:
                cur_phase[0] = 5
                r1rd_tgt[mc] = emit_r1rd(mc, [("ss_r1%d" % (mc % 2), 16 * (mc // 2 + 1))])
                cur_phase[0] = 4
            if mc + 2 < DC:
                emit_xm(mc + 2)
        em("t", pending_stats)

        def f_v_fin1(v):
            v.wait_ge(SM["t_stats"], DC)
            v.wait_ge(SM["s_linvld"], 16)
            v.wait_ge(SM["v_norm"], 1)
            v.wait_ge(SM["s_linv"], 16)
            vsync(v, v.tensor_copy(sv(0), P45[0:1, 0:1024]))
            vsync(v, v.tensor_copy(sv(1), P67[0:1, 0:1024]))
            vsync(v, v.tensor_scalar_mul(sv(2), sv(0), 1.0 / D))
            vsync(v, v.tensor_scalar_mul(sv(4), sv(1), 1.0 / D))
            vsync(v, v.tensor_mul(sv(5), sv(2), sv(2)))
            vsync(v, v.tensor_sub(sv(4), sv(4), sv(5)))
            v.tensor_scalar_add(sv(4), sv(4), EPS).then_inc(SM["v_stat"], 1)
        em("v", f_v_fin1)

        def f_c_fin1(sc):
            sc.wait_ge(SM["v_stat"], 1)
            sc.activation(sv(3), sv(4), AF.Sqrt, bias=0.0, scale=1.0).then_inc(SM["s_sqrt"], 1)
        em("c", f_c_fin1)

        def f_v_fin1b(v):
            v.wait_ge(SM["s_sqrt"], 1)
            v.reciprocal(sv(3), sv(3)).then_inc(SM["v_stat"], 1)
        em("v", f_v_fin1b)

        def f_s_fin1(sy):
            sy.wait_ge(SM["v_stat"], 2)
            sy.dma_start(out=d_vec4[0:1, :], in_=sv(2)).then_inc(SM["s_vec4"], 16)
            sy.dma_start(out=d_vec4[1:2, :], in_=sv(3)).then_inc(SM["s_vec4"], 16)
            sy.wait_ge(SM["s_vec4"], 32)
            sy.dma_start(out=sb_bcm1[:], in_=d_vec4[0:1, :].broadcast_to([128, TOK])).then_inc(SM["s_bc"], 16)
            sy.dma_start(out=sb_bca1[:], in_=d_vec4[1:2, :].broadcast_to([128, TOK])).then_inc(SM["s_bc"], 16)
        em("s", f_s_fin1)

        cur_phase[0] = 5
        def f_g_lat(g):
            g.wait_ge(SM["t_mmpc"], T_MMPC_TOTAL)
            out = sb_latb.rearrange("p (k q) -> p k q", k=DC)
            g.dma_start(out=out, in_=d_lat[:]).then_inc(SM["s_lat"], 16)
        em("g", f_g_lat)

        def f_s_lbt(sy):
            sy.wait_ge(SM["t_mmpc"], T_MMPC_TOTAL)
            sy.wait_ge(SM["v_nmul"], H)
            sy.dma_start(out=sb_lbt32[:], in_=d_lbt[:]).then_inc(SM["s_lbt"], 16)
        em("s", f_s_lbt)

        def f_v_lbt(v):
            v.wait_ge(SM["s_lbt"], 16)
            v.wait_ge(SM["s_ini"], S_INI_TOTAL)
            v.tensor_scalar(sb_lbtb[:], sb_lbt32[:], sb_lw[:, 0:1], LORA_SCALE,
                            ALU.mult, ALU.mult).then_inc(SM["v_lbt"], 1)
        em("v", f_v_lbt)

        for mc in range(4, DC):
            r1rd_tgt[mc] = emit_r1rd(mc, [("v_ysub", mc - 3),
                                          ("ss_r1%d" % (mc % 2), 16 * (mc // 2 + 1))])
        def yb_sub(v, mc):
            slot = mc % 4
            v.wait_ge(SM["sr1rd%d" % slot], r1rd_tgt[mc])
            if mc == 0:
                v.wait_ge(SM["s_bc"], 32)
            if mc >= 6:
                v.wait_ge(SM["v_y"], mc - 5)
            v.tensor_sub(ytmp(mc)[:], vf32(OFF_R1RD[slot], TOK)[:], sb_bcm1[:]).then_inc(SM["v_ysub"], 1)

        def yb_mul(v, mc):
            v.wait_ge(SM["v_ysub"], mc + 1)
            v.tensor_mul(sb_ybf[:, mc * TOK:(mc + 1) * TOK], ytmp(mc)[:], sb_bca1[:]).then_inc(SM["v_y"], 1)

        def f_v_yb(v):
            for mc in range(DC):
                yb_sub(v, mc)
                if mc >= 3:
                    yb_mul(v, mc - 3)
            for mc in range(DC - 3, DC):
                yb_mul(v, mc)
        em("v", f_v_yb)

        def f_t_h(t):
            t.wait_ge(SM["s_lat"], 16)
            t.wait_ge(SM["v_y"], DC)
            t.wait_ge(SM["v_r1"], DC)
            for kc in range(DC):
                st = sb_latb[:, kc * 128:(kc + 1) * 128]
                t.matmul(P01[:, 0:512], st, sb_ybf[:, kc * TOK: kc * TOK + 512],
                         start=(kc == 0), stop=(kc == DC - 1))
                if kc == DC - 1:
                    t.matmul(P01[:, 512:1024], st, sb_ybf[:, kc * TOK + 512:(kc + 1) * TOK],
                             start=False, stop=True).then_inc(SM["t_mmh"], 1)
                else:
                    t.matmul(P01[:, 512:1024], st, sb_ybf[:, kc * TOK + 512:(kc + 1) * TOK],
                             start=(kc == 0), stop=False)
        em("t", f_t_h)

        def f_v_h(v):
            v.wait_ge(SM["t_mmh"], 1)
            v.tensor_copy(sb_hb[:], P01[:]).then_inc(SM["v_evh"], 1)
        em("v", f_v_h)

        wt_tgt = [None] * DC
        wt_tgt[0] = emit_wblk_dma(d_wt[0], 0, [("t_mmwo", DC), ("t_mmh", 1)])
        wt_tgt[1] = emit_wblk_dma(d_wt[1], 1, [("t_mmwo", DC), ("t_mmh", 1)])

        pending_stats2 = None
        for mc in range(DC):
            slot = mc % 2
            r1rd_tgt[DC + mc] = emit_r1rd(
                DC + mc, [("v_r2", mc - 3)] if mc >= 4 else [("v_ysub", DC)], eng="g")
            if mc + 2 < DC:
                wt_tgt[mc + 2] = emit_wblk_dma(d_wt[mc + 2], mc + 2, [("t_mmf", mc + 1)])

            def f_t(t, mc=mc, slot=slot):
                t.wait_ge(SM["gw%d" % slot], wt_tgt[mc])
                if mc == 0:
                    t.wait_ge(SM["v_evh"], 1)
                    t.wait_ge(SM["v_lbt"], 1)
                if mc >= 2:
                    t.wait_ge(SM["v_r2"], mc - 1)
                pair = P[mc % 2][:]
                wb = vbf(OFF_WBLK[slot], DC * 128)
                for kc in range(DC):
                    st = wb[:, kc * 128:(kc + 1) * 128]
                    t.matmul(pair[:, 0:512], st, sb_ybf[:, kc * TOK: kc * TOK + 512],
                             start=(kc == 0), stop=False)
                    t.matmul(pair[:, 512:1024], st, sb_ybf[:, kc * TOK + 512:(kc + 1) * TOK],
                             start=(kc == 0), stop=False)
                stl = sb_lbtb[:, mc * 128:(mc + 1) * 128]
                t.matmul(pair[:, 0:512], stl, sb_hb[:, 0:512], start=False, stop=True)
                t.matmul(pair[:, 512:1024], stl, sb_hb[:, 512:1024], start=False, stop=True).then_inc(SM["t_mmf"], 1)
            em("t", f_t)
            if pending_stats2 is not None:
                em("t", pending_stats2)

            def f_t_stats(t, mc=mc):
                t.wait_ge(SM["s_st2"], 2 * (mc + 1))
                if mc == 0:
                    t.wait_ge(SM["v_stat"], 2)
                t.matmul(P45[0:1, 0:512], sb_ones[:], sb_r1bf[:, 0:512],
                         start=(mc == 0), stop=(mc == DC - 1))
                t.matmul(P45[0:1, 512:1024], sb_ones[:], sb_r1bf[:, 512:1024],
                         start=(mc == 0), stop=(mc == DC - 1))
                t.matmul(P67[0:1, 0:512], sb_ones[:], sb_sqbf[:, 0:512],
                         start=(mc == 0), stop=(mc == DC - 1))
                t.matmul(P67[0:1, 512:1024], sb_ones[:], sb_sqbf[:, 512:1024],
                         start=(mc == 0), stop=(mc == DC - 1)).then_inc(SM["t_st2"], 1)

            def f_v(v, mc=mc, slot=slot):
                rslot = (DC + mc) % 4
                v.wait_ge(SM["gr1rd%d" % rslot], r1rd_tgt[DC + mc])
                if mc >= 1:
                    v.wait_ge(SM["v_r2"], mc)
                vsync(v, v.tensor_sub(sb_y32[:], vf32(OFF_R1RD[rslot], TOK)[:], sb_bcm1[:]))
                vsync(v, v.tensor_mul(sb_y32[:], sb_y32[:], sb_bca1[:]))
                v.wait_ge(SM["t_mmf"], mc + 1)
                if mc < 2:
                    v.wait_ge(SM["v_r1"], DC)
                if mc >= 2:
                    v.wait_ge(SM["ss_r2%d" % slot], 16 * (mc // 2))
                    v.wait_ge(SM["s_st2"], 2 * (mc - 1))
                pair = P[mc % 2][:]
                r2b = vf32(OFF_R2BUF[slot], TOK)
                v.scalar_tensor_tensor(r2b[:, 0:512], pair[:, 0:512], sb_bi[:, mc:mc + 1],
                                       sb_y32[:, 0:512], ALU.add, ALU.add)
                v.scalar_tensor_tensor(r2b[:, 512:1024], pair[:, 512:1024], sb_bi[:, mc:mc + 1],
                                       sb_y32[:, 512:1024], ALU.add, ALU.add).then_inc(SM["v_r2"], 1)
            em("v", f_v)

            def f_c(sc, mc=mc, slot=slot):
                sc.wait_ge(SM["v_r2"], mc + 1)
                if mc >= 1:
                    sc.wait_ge(SM["t_st2"], mc)
                if mc == 0:
                    sc.wait_ge(SM["t_stats"], DC)
                r2b = vf32(OFF_R2BUF[slot], TOK)
                sc.copy(sb_r1bf[:], r2b[:])
                sc.activation(sb_sqbf[:], r2b[:], AF.Square, bias=0.0, scale=1.0).then_inc(SM["s_st2"], 2)
            em("c", f_c)
            pending_stats2 = f_t_stats

            def f_s2(sy, mc=mc, slot=slot):
                sy.wait_ge(SM["v_r2"], mc + 1)
                sy.dma_start(out=d_r2[mc], in_=vf32(OFF_R2BUF[slot], TOK)).then_inc(SM["ss_r2%d" % slot], 16)
            em("s", f_s2)
        em("t", pending_stats2)

        def f_v_fin2(v):
            v.wait_ge(SM["t_st2"], DC)
            vsync(v, v.tensor_copy(sv(0), P45[0:1, 0:1024]))
            vsync(v, v.tensor_copy(sv(1), P67[0:1, 0:1024]))
            vsync(v, v.tensor_scalar_mul(sv(2), sv(0), 1.0 / D))
            vsync(v, v.tensor_scalar_mul(sv(4), sv(1), 1.0 / D))
            vsync(v, v.tensor_mul(sv(5), sv(2), sv(2)))
            vsync(v, v.tensor_sub(sv(4), sv(4), sv(5)))
            v.tensor_scalar_add(sv(4), sv(4), EPS).then_inc(SM["v_stat2"], 1)
        em("v", f_v_fin2)

        def f_c_fin2(sc):
            sc.wait_ge(SM["v_stat2"], 1)
            sc.activation(sv(3), sv(4), AF.Sqrt, bias=0.0, scale=1.0).then_inc(SM["s_sqrt2"], 1)
        em("c", f_c_fin2)

        def f_v_fin2b(v):
            v.wait_ge(SM["s_sqrt2"], 1)
            v.reciprocal(sv(3), sv(3)).then_inc(SM["v_stat2"], 1)
        em("v", f_v_fin2b)

        def f_s_fin2(sy):
            sy.wait_ge(SM["v_stat2"], 2)
            sy.dma_start(out=d_vec4[2:3, :], in_=sv(2)).then_inc(SM["s_vec4"], 16)
            sy.dma_start(out=d_vec4[3:4, :], in_=sv(3)).then_inc(SM["s_vec4"], 16)
            sy.wait_ge(SM["s_vec4"], 64)
            sy.dma_start(out=sb_bcm2[:], in_=d_vec4[2:3, :].broadcast_to([128, TOK])).then_inc(SM["s_bc"], 16)
            sy.dma_start(out=sb_bca2[:], in_=d_vec4[3:4, :].broadcast_to([128, TOK])).then_inc(SM["s_bc"], 16)
        em("s", f_s_fin2)

        cur_phase[0] = 6
        r2rd_tgt6 = [16 * (mc // 4 + 1) for mc in range(DC)]

        def emit_r2rd(mc):
            slot = mc % 4

            def f_s(sy, mc=mc, slot=slot):
                sy.wait_ge(SM["ss_r2%d" % (mc % 2)], 16 * (mc // 2 + 1))
                if slot >= 2 and mc < 4:
                    sy.wait_ge(SM["t_mmf"], DC)
                if mc >= 4:
                    sy.wait_ge(SM["v_osub"], mc - 3)
                sy.dma_start(out=vf32(OFF_R2RD[slot], TOK), in_=d_r2[mc]).then_inc(SM["sr2rd%d" % slot], 16)
            em("g", f_s)

        for mc in range(4):
            emit_r2rd(mc)

        def o_sub(v, mc):
            slot = mc % 4
            v.wait_ge(SM["sr2rd%d" % slot], r2rd_tgt6[mc])
            if mc == 0:
                v.wait_ge(SM["s_bc"], 64)
            if mc >= 6:
                v.wait_ge(SM["v_out"], mc - 5)
            v.tensor_sub(ytmp(mc)[:], vf32(OFF_R2RD[slot], TOK)[:], sb_bcm2[:]).then_inc(SM["v_osub"], 1)

        def o_mul(v, mc):
            v.wait_ge(SM["v_osub"], mc + 1)
            if mc >= 2:
                v.wait_ge(SM["so_out%d" % (mc % 2)], 16 * (mc // 2))
            v.tensor_mul(vf32(OFF_OUTB[mc % 2], TOK)[:], ytmp(mc)[:], sb_bca2[:]).then_inc(SM["v_out"], 1)

        def f_v6(v):
            for mc in range(DC):
                o_sub(v, mc)
                if mc >= 3:
                    o_mul(v, mc - 3)
            for mc in range(DC - 3, DC):
                o_mul(v, mc)
        em("v", f_v6)

        for mc in range(DC):

            def f_s2(sy, mc=mc):
                sy.wait_ge(SM["v_out"], mc + 1)
                sy.dma_start(out=d_out[mc], in_=vf32(OFF_OUTB[mc % 2], TOK)).then_inc(SM["so_out%d" % (mc % 2)], 16)
            em("s", f_s2)
            if mc + 4 < DC:
                emit_r2rd(mc + 4)

        def f_s_final(sy):
            sy.wait_ge(SM["so_out0"], 16 * (DC // 2))
            sy.wait_ge(SM["so_out1"], 16 * (DC // 2))
        em("s", f_s_final)

        dma_table = [
            (0, "s_ini", 48),
            (1, "gx0", 128), (1, "gx1", 128), (1, "s_wkv", 16), (1, "s_xmbf", 64),
            (1, "sc_st0", 128), (1, "sc_st1", 128), (1, "s_kvrb", 384),
            (2, "gw0", 256), (2, "gw1", 256),
            (2, "ss_qout0", 256), (2, "ss_qout1", 256),
            (3, "s_masks", 32), (3, "sq_qh0", 256), (3, "sq_qh1", 256),
            (3, "ss_lt0", 256), (3, "ss_lt1", 256), (3, "s_linvld", 16),
            (3, "s_linv", 16), (3, "ss_sm0", 128), (3, "ss_sm1", 128),
            (3, "ss_sm2", 128), (3, "ss_sm3", 128),
            (4, "gw0", 256), (4, "gw1", 256),
            (4, "sx_xm0", 256), (4, "sx_xm1", 256),
            (4, "ss_r10", 256), (4, "ss_r11", 256),
            (4, "s_vec4", 32), (4, "s_bc", 32),
            (5, "gw0", 256), (5, "gw1", 256),
            (5, "s_lat", 16), (5, "s_lbt", 16),
            (5, "sr1rd0", 128), (5, "sr1rd1", 128),
            (5, "sr1rd2", 128), (5, "sr1rd3", 128),
            (5, "gr1rd0", 128), (5, "gr1rd1", 128),
            (5, "gr1rd2", 128), (5, "gr1rd3", 128),
            (5, "ss_r20", 256), (5, "ss_r21", 256),
            (5, "s_vec4", 32), (5, "s_bc", 32),
            (6, "sr2rd0", 128), (6, "sr2rd1", 128),
            (6, "sr2rd2", 128), (6, "sr2rd3", 128),
            (6, "so_out0", 256), (6, "so_out1", 256),
        ]
        fin = {}
        for ph, sn, inc in dma_table:
            if ph <= PHASES:
                fin[sn] = fin.get(sn, 0) + inc

        def f_dma_fin(sy):
            for sn in sorted(fin):
                sy.wait_ge(SM[sn], fin[sn])
        prog["s"].append((0, f_dma_fin))

        with nc.Block() as block:
            @block.gpsimd
            def _(g):
                for ph, f in prog["g"]:
                    if ph <= PHASES:
                        f(g)

            @block.sync
            def _(sy):
                for ph, f in prog["s"]:
                    if ph <= PHASES:
                        f(sy)

            @block.tensor
            def _(t):
                for ph, f in prog["t"]:
                    if ph <= PHASES:
                        f(t)

            @block.vector
            def _(v):
                for ph, f in prog["v"]:
                    if ph <= PHASES:
                        f(v)

            @block.scalar
            def _(sc):
                for ph, f in prog["c"]:
                    if ph <= PHASES:
                        f(sc)

    return nc


def _stage_inputs(L0_attn_x, L0_attn_Wq, L0_attn_Wkv, L0_attn_Wo, L0_attn_bo,
                  L0_ffn_lora_A, L0_ffn_lora_B, L0_ffn_W, L0_ffn_bias,
                  L0_ffn_lora_weights):
    f32 = np.float32
    x = np.asarray(L0_attn_x, f32)
    Wq = np.asarray(L0_attn_Wq, f32)
    Wkv = np.asarray(L0_attn_Wkv, f32)
    Wo = np.asarray(L0_attn_Wo, f32)
    bo = np.asarray(L0_attn_bo, f32)
    lA = np.asarray(L0_ffn_lora_A, f32)
    lB = np.asarray(L0_ffn_lora_B, f32)
    W = np.asarray(L0_ffn_W, f32)
    bi = np.asarray(L0_ffn_bias, f32)
    lw = np.asarray(L0_ffn_lora_weights, f32)

    xT = x.reshape(TOKALL, D).T
    xs_all = np.ascontiguousarray(
        xT.reshape(DC, 128, 2 * NXT, XTW).transpose(2, 1, 0, 3))
    wq = np.ascontiguousarray(
        Wq.reshape(DC, 128, DC, 128).transpose(2, 1, 0, 3))
    wo = np.ascontiguousarray(
        Wo.reshape(DC, 128, DC, 128).transpose(2, 1, 0, 3))
    wt = np.ascontiguousarray(
        W.reshape(DC, 128, DC, 128).transpose(0, 3, 2, 1))
    lat = np.ascontiguousarray(
        lA.reshape(128, D).T.reshape(DC, 128, 128).transpose(1, 0, 2))
    lbt = np.ascontiguousarray(lB.transpose(0, 2, 1).reshape(128, D))
    lwr = np.ascontiguousarray(np.repeat(lw, 16).reshape(128, 1))
    bo_c = np.ascontiguousarray(bo.reshape(DC, 128).T)
    bi_c = np.ascontiguousarray(bi.reshape(DC, 128).T)

    wkv_by_bq = []
    for bq in range(B):
        sl = Wkv[:, 512 * bq:512 * (bq + 1)]
        perm = np.concatenate([np.arange(0, 128), np.arange(256, 384),
                               np.arange(128, 256), np.arange(384, 512)])
        slr = sl[:, perm]
        wkv_by_bq.append(np.ascontiguousarray(
            slr.reshape(DC, 128, 512).transpose(1, 0, 2)))

    tri = np.where(np.arange(128)[:, None] <= np.arange(128)[None, :],
                   np.float32(0.0), np.float32(MASK_NEG))
    dead = np.full((128, 128), MASK_NEG, f32)
    full = np.zeros((128, 128), f32)

    in_maps = []
    rows_by_core = []
    for c in range(8):
        bq = c // 2
        odd = c % 2
        rows = np.concatenate([np.arange(128) + 128 * (2 * s + odd) for s in range(QTS)])
        rows_by_core.append(rows)
        xm = np.ascontiguousarray(
            x[bq][rows, :].T.reshape(DC, 128, TOK).transpose(1, 0, 2))
        mk = np.empty((KT, 128, 128), f32)
        for kc in range(KT):
            s = kc // 2
            gidx = 2 * s + odd
            if kc < gidx:
                mk[kc] = full
            elif kc == gidx:
                mk[kc] = tri
            else:
                mk[kc] = dead
        masks = np.ascontiguousarray(mk.transpose(2, 0, 1))
        xs = xs_all[odd * NXT:(odd + 1) * NXT]
        in_maps.append({
            "xs": xs, "xm": xm, "wq": wq, "wo": wo, "wt": wt,
            "wkv": wkv_by_bq[bq], "lat": lat, "lbt": lbt, "lw": lwr,
            "bo": bo_c, "bi": bi_c, "masks": masks,
            "iden": np.eye(128, dtype=f32),
        })
    return in_maps, rows_by_core


def kernel(**inputs):
    in_maps, rows_by_core = _stage_inputs(**inputs)
    if "nc" not in _CACHED:
        _CACHED["nc"] = build_nc()
    nc = _CACHED["nc"]
    res = run_bass_kernel_spmd(nc, in_maps, core_ids=list(range(8)))
    out = np.empty((B, T, D), np.float32)
    for c in range(8):
        y_t = np.asarray(res.results[c]["out"]).reshape(D, TOK)
        out[c // 2, rows_by_core[c], :] = y_t.T
    return out


# version 09 (final)
# speedup vs baseline: 1.1083x; 1.1083x over previous
import numpy as np
import ml_dtypes
from contextlib import ExitStack

import concourse.bass as bass
import concourse.mybir as mybir
from concourse.bass_utils import run_bass_kernel_spmd

F32 = mybir.dt.float32
BF16 = mybir.dt.bfloat16
U8 = mybir.dt.uint8
AF = mybir.ActivationFunctionType
ALU = mybir.AluOpType

D = 4096
DC = 32
B = 4
T = 2048
TOKALL = B * T
TOK = 1024
H = 32
HD = 128
NXT = 16
XTW = 256
KT = 16
QTS = 8

EXP_BIAS = -12.0
EXP_SCALE = 1.0 / float(np.sqrt(128.0))
EPS = 1e-5
MASK_NEG = -1.0e30
LORA_SCALE = 2.0

ARENA_BYTES = 210944

OFF_XT = (0, 16384)
OFF_MASKS = 0
OFF_EXPS = (4096, 6144, 8192, 10240, 12288)
OFF_QH = (14336, 16384)
OFF_LATB = 0
OFF_LBTB = 8192
OFF_HB = 16384
OFF_R1RD = (18432, 22528, 90112, 94208)
OFF_Y32 = 26624
OFF_R2RD = (0, 4096, 8192, 12288)
OFF_OUTB = (16384, 20480)

OFF_WKV = 32768
OFF_WBLK = (32768, 40960)
OFF_QOUT = (49152, 51200)
OFF_XM32 = (49152, 53248)
OFF_R2BUF = (49152, 53248)
OFF_R1BUF = (57344, 61440)

OFF_KT = 65536
OFF_SV = tuple(65536 + i * 4096 for i in range(6))
OFF_LINV32 = 65536
OFF_LTMP = (196928, 201024)
OFF_V = 98304
OFF_SOFTB = (98304, 102400, 106496, 110592)
OFF_BCM1 = 98304
OFF_BCA1 = 102400
OFF_BCM2 = 106496
OFF_BCA2 = 110592
OFF_LBT32 = 114688

OFF_XMBF = 131072
OFF_CTX = 131072
OFF_YBF = 131072

OFF_CONSTS = 196608
OFF_ONESB = 196640
OFF_BO = 196672
OFF_BI = 196800
OFF_LALL = 196928
OFF_SVEC = 201024
OFF_R1BF = 205120
OFF_SQBF = 207168
OFF_IDEN = 209216

_CACHED = {}
PHASES = 6


def build_nc():
    nc = bass.Bass("TRN2", target_bir_lowering=False)

    d_xs = nc.dram_tensor("xs", [NXT, 128, DC, XTW], F32, kind="ExternalInput").ap()
    d_cin = nc.dram_tensor("cin", [NXT, 128, 1024], BF16, kind="Internal").ap()
    d_cout_a = nc.dram_tensor("cout_a", [2, 8, 128, 1024], BF16, kind="Internal").ap()
    d_cout_b = nc.dram_tensor("cout_b", [2, 8, 128, 1024], BF16, kind="Internal").ap()
    d_xm = nc.dram_tensor("xm", [128, DC, TOK], F32, kind="ExternalInput").ap()
    d_wq = nc.dram_tensor("wq", [DC, 128, DC, 128], F32, kind="ExternalInput").ap()
    d_wo = nc.dram_tensor("wo", [DC, 128, DC, 128], F32, kind="ExternalInput").ap()
    d_wt = nc.dram_tensor("wt", [DC, 128, DC, 128], F32, kind="ExternalInput").ap()
    d_wkv = nc.dram_tensor("wkv", [128, DC, 512], F32, kind="ExternalInput").ap()
    d_lat = nc.dram_tensor("lat", [128, DC, 128], F32, kind="ExternalInput").ap()
    d_lbt = nc.dram_tensor("lbt", [128, D], F32, kind="ExternalInput").ap()
    d_lw = nc.dram_tensor("lw", [128, 1], F32, kind="ExternalInput").ap()
    d_bo = nc.dram_tensor("bo", [128, DC], F32, kind="ExternalInput").ap()
    d_bi = nc.dram_tensor("bi", [128, DC], F32, kind="ExternalInput").ap()
    d_masks = nc.dram_tensor("masks", [128, KT, 128], F32, kind="ExternalInput").ap()
    d_iden = nc.dram_tensor("iden", [128, 128], F32, kind="ExternalInput").ap()

    d_out = nc.dram_tensor("out", [DC, 128, TOK], F32, kind="ExternalOutput").ap()

    d_qsc = nc.dram_tensor("qsc", [H, 128, TOK], BF16, kind="Internal").ap()
    d_ctxsc = nc.dram_tensor("ctxsc", [H, 128, TOK], BF16, kind="Internal").ap()
    d_r1 = nc.dram_tensor("r1sc", [DC, 128, TOK], F32, kind="Internal").ap()
    d_r2 = nc.dram_tensor("r2sc", [DC, 128, TOK], F32, kind="Internal").ap()
    d_linv = nc.dram_tensor("linv", [H, TOK], F32, kind="Internal").ap()
    d_vec4 = nc.dram_tensor("vec4", [4, TOK], F32, kind="Internal").ap()

    arena = nc.alloc_sbuf_tensor("arena", [128, ARENA_BYTES], U8).ap()

    def vbf(off, cols, parts=128):
        return arena[0:parts, off:off + 2 * cols].bitcast(BF16)

    def vf32(off, cols, parts=128):
        return arena[0:parts, off:off + 4 * cols].bitcast(F32)

    sb_consts = vf32(OFF_CONSTS, 4)
    sb_ones = vbf(OFF_ONESB, 1)
    sb_bo = vf32(OFF_BO, DC)
    sb_bi = vf32(OFF_BI, DC)
    sb_linv32 = arena[0:32, OFF_LINV32:OFF_LINV32 + 4 * TOK].bitcast(F32)

    def sv(i):
        return arena[0:1, OFF_SV[i]:OFF_SV[i] + 4 * TOK].bitcast(F32)

    def ytmp(i):
        return vf32(OFF_SV[i % 6], TOK)

    def ltmp(i):
        return arena[0:1, OFF_LTMP[i]:OFF_LTMP[i] + 4 * TOK].bitcast(F32)
    sb_r1bf = vbf(OFF_R1BF, TOK)
    sb_sqbf = vbf(OFF_SQBF, TOK)
    sb_wkv = vbf(OFF_WKV, DC * 512)
    sb_kt = vbf(OFF_KT, 2 * B * T // 1)
    sb_v = vbf(OFF_V, 16384)
    sb_masks = vbf(OFF_MASKS, KT * 128)
    sb_iden = vbf(OFF_IDEN, 128)
    sb_xmbf = vbf(OFF_XMBF, DC * TOK)
    sb_ctx = vbf(OFF_CTX, H * TOK)
    sb_ybf = vbf(OFF_YBF, DC * TOK)
    sb_latb = vbf(OFF_LATB, DC * 128)
    sb_lbtb = vbf(OFF_LBTB, D)
    sb_lbt32 = vf32(OFF_LBT32, D)
    sb_hb = vbf(OFF_HB, TOK)
    sb_y32 = vf32(OFF_Y32, TOK)
    sb_bcm1 = vf32(OFF_BCM1, TOK)
    sb_bca1 = vf32(OFF_BCA1, TOK)
    sb_bcm2 = vf32(OFF_BCM2, TOK)
    sb_bca2 = vf32(OFF_BCA2, TOK)

    prog = {"g": [], "s": [], "t": [], "v": [], "c": []}
    cur_phase = [0]
    dma_fin = {}

    def em(e, f):
        prog[e].append((cur_phase[0], f))

    VSEQ = [0]

    def vsync(v, inst):
        VSEQ[0] += 1
        inst.then_inc(SM["v_seq"], 1)
        v.wait_ge(SM["v_seq"], VSEQ[0])

    def note_dma(semname, inc=16):
        key = semname
        dma_fin[key] = dma_fin.get(key, 0) + inc
        dma_fin.setdefault("_byphase", {}).setdefault(cur_phase[0], {})
        ph = dma_fin["_byphase"][cur_phase[0]]
        ph[key] = ph.get(key, 0) + inc

    with ExitStack() as es:
        P = [es.enter_context(nc.psum_tensor(f"P{i}", [128, 1024], F32)) for i in range(4)]
        P01, P23, P45, P67 = [p[:] for p in P]

        semnames = [
            "gx0", "gx1", "gw0", "gw1", "s_wkv", "s_xmbf", "s_lat", "s_lbt",
            "s_ini", "s_masks", "s_linv", "s_vec4", "s_bc",
            "ss_sm0", "ss_sm1", "ss_sm2", "ss_sm3", "sq_qh0", "sq_qh1", "ss_qout0", "ss_qout1",
            "ss_lt0", "ss_lt1", "s_linvld", "sc_st0", "sc_st1", "s_coll", "s_kvrb", "s_kvrb2",
            "sx_xm0", "sx_xm1", "ss_r10", "ss_r11", "sr1rd0", "sr1rd1", "sr1rd2", "sr1rd3",
            "gr1rd0", "gr1rd1", "gr1rd2", "gr1rd3",
            "ss_r20", "ss_r21", "sr2rd0", "sr2rd1", "sr2rd2", "sr2rd3", "so_out0", "so_out1",
            "v_ini", "t_ph1", "v_ev1", "t_mmq", "v_evq", "t_mms", "v_msk",
            "s_exp", "t_mmpc", "v_evl", "v_evctx", "v_norm", "v_nmul",
            "t_mmwo", "v_r1", "s_stats", "t_stats", "v_stat", "s_sqrt",
            "v_y", "t_mmh", "v_evh", "v_lbt", "t_mmf", "v_r2", "s_st2",
            "t_st2", "v_stat2", "s_sqrt2", "v_out", "v_seq", "v_ysub", "v_osub",
            "v_accbf", "t_lf", "g_acc", "g_seq", "v_acc", "sx_ct0", "sx_ct1",
            "g_crd0", "g_crd1", "g_crd2", "g_crd3",
        ]
        SM = {n: es.enter_context(nc.semaphore(n)) for n in semnames}

        def f_v_init(v):
            v.memset(sb_consts[:, 0:1], EXP_BIAS)
            v.memset(sb_consts[:, 1:2], EXP_SCALE)
            v.memset(sb_ones[:], 1.0).then_inc(SM["v_ini"], 1)
        em("v", f_v_init)

        sb_lw = arena[0:128, OFF_CONSTS + 8:OFF_CONSTS + 12].bitcast(F32)

        def f_s_init(sy):
            sy.dma_start(out=sb_bo[:], in_=d_bo[:]).then_inc(SM["s_ini"], 16)
            sy.dma_start(out=sb_bi[:], in_=d_bi[:]).then_inc(SM["s_ini"], 16)
            sy.dma_start(out=sb_lw[:], in_=d_lw[:]).then_inc(SM["s_ini"], 16)
        em("s", f_s_init)
        S_INI_TOTAL = 48

        cur_phase[0] = 1
        gxc = [0, 0]

        def emit_xt_dma(i):
            slot = i % 2
            gxc[slot] += 1
            tgt = 16 * gxc[slot]
            def f(g):
                if i >= 2:
                    g.wait_ge(SM["t_ph1"], 4 * (i - 1))
                out = vbf(OFF_XT[slot], DC * XTW).rearrange("p (c t) -> p c t", c=DC)
                g.dma_start(out=out, in_=d_xs[i]).then_inc(SM["gx%d" % slot], 16)
            em("g", f)
            return tgt

        def f_iden(g):
            g.dma_start(out=sb_iden[:], in_=d_iden[:]).then_inc(SM["s_masks"], 16)
        em("g", f_iden)

        xt_tgt = [None] * NXT
        xt_tgt[0] = emit_xt_dma(0)

        def f_wkv(g):
            out = sb_wkv.rearrange("p (c w) -> p c w", c=DC)
            g.dma_start(out=out, in_=d_wkv[:]).then_inc(SM["s_wkv"], 16)
        em("g", f_wkv)
        def f_xmbf(g, q):
            out = sb_xmbf.rearrange("p (c t) -> p c t", c=DC)
            g.dma_start(out=out[:, q * 8:(q + 1) * 8, :],
                        in_=d_xm[:, q * 8:(q + 1) * 8, :]).then_inc(SM["s_xmbf"], 16)
        nxt_i = 1
        for q in range(4):
            while nxt_i < 5 + q * 3:
                xt_tgt[nxt_i] = emit_xt_dma(nxt_i)
                nxt_i += 1
            em("g", lambda g, q=q: f_xmbf(g, q))
        while nxt_i < NXT:
            xt_tgt[nxt_i] = emit_xt_dma(nxt_i)
            nxt_i += 1

        def emit_ph1_tensor():
            for i in range(NXT):
                slot = i % 2
                tgt = xt_tgt[i]
                def f(t, i=i, slot=slot, tgt=tgt):
                    t.wait_ge(SM["gx%d" % slot], tgt)
                    if i == 0:
                        t.wait_ge(SM["s_wkv"], 16)
                    xt = vbf(OFF_XT[slot], DC * XTW)
                    for gl in range(2):
                        pair = P[gl][:]
                        if i >= 1:
                            t.wait_ge(SM["v_ev1"], 4 * (i - 1) + gl + 1)
                        for kc in range(DC):
                            mm = t.matmul(pair[:, 0:XTW],
                                          sb_wkv[:, kc * 512 + gl * 128: kc * 512 + gl * 128 + 128],
                                          xt[:, kc * XTW:(kc + 1) * XTW],
                                          start=(kc == 0), stop=(kc == DC - 1))
                            if kc == DC - 1:
                                mm.then_inc(SM["t_ph1"], 1)
                    for s in range(2):
                        pair = P[2 + s][:]
                        if i >= 1:
                            t.wait_ge(SM["v_ev1"], 4 * (i - 1) + 2 + s + 1)
                        for kc in range(DC):
                            mm = t.matmul(pair[:, 0:256],
                                          xt[:, kc * XTW + s * 128: kc * XTW + s * 128 + 128],
                                          sb_wkv[:, kc * 512 + 256: kc * 512 + 512],
                                          start=(kc == 0), stop=(kc == DC - 1))
                            if kc == DC - 1:
                                mm.then_inc(SM["t_ph1"], 1)
                em("t", f)
        emit_ph1_tensor()

        STAG = (196928, 198976)
        def emit_ph1_vector():
            for i in range(NXT):
                def f(v, i=i):
                    stag = vbf(STAG[i % 2], 1024)
                    if i >= 2:
                        v.wait_ge(SM["sc_st%d" % (i % 2)], 16 * (i // 2))
                    for gl in range(2):
                        v.wait_ge(SM["t_ph1"], 4 * i + gl + 1)
                        v.tensor_copy(stag[:, gl * XTW:(gl + 1) * XTW],
                                      P[gl][:][:, 0:XTW]).then_inc(SM["v_ev1"], 1)
                    for s in range(2):
                        v.wait_ge(SM["t_ph1"], 4 * i + 2 + s + 1)
                        v.tensor_copy(stag[:, 512 + s * 256: 512 + (s + 1) * 256],
                                      P[2 + s][:][:, 0:256]).then_inc(SM["v_ev1"], 1)
                em("v", f)

                def f_s(sy, i=i):
                    sy.wait_ge(SM["v_ev1"], 4 * (i + 1))
                    sy.dma_start(out=d_cin[i], in_=vbf(STAG[i % 2], 1024)).then_inc(SM["sc_st%d" % (i % 2)], 16)
                em("s", f_s)
        emit_ph1_vector()
        T_PH1_TOTAL = 4 * NXT
        V_EV1_TOTAL = 4 * NXT

        RG = [[0, 1], [2, 3], [4, 5], [6, 7]]

        def f_coll_a(g):
            g.wait_ge(SM["sc_st0"], 16 * 4)
            g.wait_ge(SM["sc_st1"], 16 * 4)
            g.collective_compute("AllGather", mybir.AluOpType.bypass,
                                 ins=[d_cin[0:8]], outs=[d_cout_a[:]],
                                 replica_groups=RG).then_inc(SM["s_coll"], 1)
        em("g", f_coll_a)

        def f_coll_b(g):
            g.wait_ge(SM["sc_st0"], 16 * 8)
            g.wait_ge(SM["sc_st1"], 16 * 8)
            g.collective_compute("AllGather", mybir.AluOpType.bypass,
                                 ins=[d_cin[8:16]], outs=[d_cout_b[:]],
                                 replica_groups=RG).then_inc(SM["s_coll"], 1)

        def emit_rb(which, coll_n):
            cout = d_cout_a if which == 0 else d_cout_b
            rbsem = "s_kvrb" if which == 0 else "s_kvrb2"

            def f(sy, which=which, coll_n=coll_n, cout=cout, rbsem=rbsem):
                sy.wait_ge(SM["s_coll"], coll_n)
                for half in range(2):
                    bk = which + 2 * half
                    for gl in range(2):
                        kdst = sb_kt[:, gl * 8192 + bk * 2048: gl * 8192 + (bk + 1) * 2048]
                        sy.dma_start(
                            out=kdst.rearrange("p (t c) -> p t c", t=8),
                            in_=cout[half][:, :, gl * 256:(gl + 1) * 256].rearrange(
                                "t p c -> p t c")).then_inc(SM[rbsem], 16)
                        vdst = sb_v[:, gl * 8192 + bk * 2048: gl * 8192 + (bk + 1) * 2048]
                        for s in range(2):
                            sy.dma_start(
                                out=vdst.rearrange("p (t s c) -> p t s c", t=8, s=2)[:, :, s, :],
                                in_=cout[half][:, :, 512 + s * 256 + gl * 128:
                                               512 + s * 256 + (gl + 1) * 128].rearrange(
                                    "t p c -> p t c")).then_inc(SM[rbsem], 16)
            em("s", f)

        emit_rb(0, 1)

        cur_phase[0] = 2
        ACC_V = (53248, 57344, 61440)
        ACC_G = 18432
        ACC_BF = 22528
        CTXT = (24576, 26624)
        CTXRD = (18432, 20480, 22528, 24576)
        GPS_KC0 = 9

        gwc = [0, 0]

        def emit_wblk_dma(src_ap, mc, waits):
            slot = mc % 2
            gwc[slot] += 1
            tgt = 16 * gwc[slot]
            def f(g, waits=tuple(waits)):
                for sn, val in waits:
                    g.wait_ge(SM[sn], val)
                out = vbf(OFF_WBLK[slot], DC * 128).rearrange("p (k q) -> p k q", k=DC)
                g.dma_start(out=out, in_=src_ap).then_inc(SM["gw%d" % slot], 16)
            em("g", f)
            return tgt

        def f_masks(g):
            g.wait_ge(SM["t_ph1"], T_PH1_TOTAL)
            out = sb_masks.rearrange("p (k q) -> p k q", k=KT)
            g.dma_start(out=out, in_=d_masks[:]).then_inc(SM["s_masks"], 16)
        em("g", f_masks)

        GSEQ = [0]

        def gsync(g, inst):
            GSEQ[0] += 1
            inst.then_inc(SM["g_seq"], 1)
            g.wait_ge(SM["g_seq"], GSEQ[0])

        qh_tgt = [16 * (h // 2 + 1) for h in range(H)]
        wq_tgt = [None] * DC
        pending_lc = []
        ACCSEQ = [0]
        ACCLAST = {0: 0, 1: 0, 2: 0}

        def emit_S(h, kc):
            gl = h // 16
            bk = (h % 16) // 4
            kvbase = gl * 8192 + bk * 2048
            slot = h % 2

            def f_t(t, h=h, kc=kc, kvbase=kvbase, slot=slot):
                Kg = KT * h + kc
                s0 = (kc // 2) * 128
                pair = P[1 + kc % 2][:]
                if kc == 0:
                    t.wait_ge(SM["s_kvrb"], 16 * 12)
                    if bk in (1, 3):
                        t.wait_ge(SM["s_kvrb2"], 16 * 12)
                    if h == 0:
                        t.wait_ge(SM["s_masks"], 32)
                    t.wait_ge(SM["sq_qh%d" % slot], qh_tgt[h])
                if Kg >= 2:
                    t.wait_ge(SM["s_exp"], Kg - 1)
                qh = vbf(OFF_QH[slot], TOK)
                st = sb_kt[:, kvbase + kc * 128: kvbase + (kc + 1) * 128]
                mk = sb_masks[:, kc * 128:(kc + 1) * 128]
                if s0 < 512:
                    t.matmul(pair[:, s0:512], st, qh[:, s0:512], start=True, stop=False)
                    t.matmul(pair[:, 512:1024], st, qh[:, 512:1024], start=True, stop=True)
                    t.matmul(pair[:, s0:s0 + 128], mk, sb_iden[:], start=False, stop=True).then_inc(SM["t_mms"], 1)
                else:
                    t.matmul(pair[:, s0:1024], st, qh[:, s0:1024], start=True, stop=False)
                    t.matmul(pair[:, s0:s0 + 128], mk, sb_iden[:], start=False, stop=True).then_inc(SM["t_mms"], 1)
            em("t", f_t)

        def emit_LC(h, kc):
            gl = h // 16
            bk = (h % 16) // 4
            kvbase = gl * 8192 + bk * 2048

            def f_t(t, h=h, kc=kc, kvbase=kvbase):
                Kg = KT * h + kc
                s0 = (kc // 2) * 128
                t.wait_ge(SM["s_exp"], Kg + 1)
                if kc == 0:
                    t.wait_ge(SM["v_evctx"], h)
                ex = vbf(OFF_EXPS[Kg % 5], TOK)
                stv = sb_v[:, kvbase + kc * 128: kvbase + (kc + 1) * 128]
                if s0 < 512:
                    t.matmul(P67[:, s0:512], stv, ex[:, s0:512],
                             start=(kc == 0), stop=(kc == 7))
                    t.matmul(P67[:, 512:1024], stv, ex[:, 512:1024],
                             start=(kc == 0), stop=(kc == KT - 1)).then_inc(SM["t_mmpc"], 1)
                else:
                    t.matmul(P67[:, s0:1024], stv, ex[:, s0:1024],
                             start=False, stop=(kc == KT - 1)).then_inc(SM["t_mmpc"], 1)
            em("t", f_t)

        def emit_lfinal(h):
            def f_t(t, h=h):
                if h == 0:
                    t.wait_ge(SM["v_ini"], 1)
                t.wait_ge(SM["v_evq"], min(h + 4, DC))
                t.wait_ge(SM["v_accbf"], h + 1)
                accbf = vbf(ACC_BF, TOK)
                t.matmul(P01[0:1, 0:512], sb_ones[:], accbf[:, 0:512], start=True, stop=True)
                t.matmul(P01[0:1, 512:1024], sb_ones[:], accbf[:, 512:1024],
                         start=True, stop=True).then_inc(SM["t_lf"], 1)
            em("t", f_t)

        wq_tgt[0] = emit_wblk_dma(d_wq[0], 0, [("t_ph1", T_PH1_TOTAL)])
        wq_tgt[1] = emit_wblk_dma(d_wq[1], 1, [("t_ph1", T_PH1_TOTAL)])
        cur_phase[0] = 1
        em("g", f_coll_b)
        cur_phase[0] = 2

        for u in range(DC + 2):
            if u < DC:
                def f_t(t, u=u):
                    t.wait_ge(SM["gw%d" % (u % 2)], wq_tgt[u])
                    if u == 0:
                        t.wait_ge(SM["s_xmbf"], 64)
                        t.wait_ge(SM["v_ev1"], 4 * NXT)
                    else:
                        t.wait_ge(SM["v_evq"], u)
                    if u >= 4:
                        t.wait_ge(SM["v_evl"], u - 3)
                    wb = vbf(OFF_WBLK[u % 2], DC * 128)
                    for kc in range(DC):
                        st = wb[:, kc * 128:(kc + 1) * 128]
                        t.matmul(P01[:, 0:512], st, sb_xmbf[:, kc * TOK: kc * TOK + 512],
                                 start=(kc == 0), stop=(kc == DC - 1))
                        if kc == DC - 1:
                            t.matmul(P01[:, 512:1024], st, sb_xmbf[:, kc * TOK + 512:(kc + 1) * TOK],
                                     start=False, stop=True).then_inc(SM["t_mmq"], 1)
                        else:
                            t.matmul(P01[:, 512:1024], st, sb_xmbf[:, kc * TOK + 512:(kc + 1) * TOK],
                                     start=(kc == 0), stop=False)
                em("t", f_t)

            if u >= 2:
                h = u - 2
                if h >= 1:
                    emit_lfinal(h - 1)
                for kc in range(KT):
                    emit_S(h, kc)
                    pending_lc.append((h, kc))
                    if len(pending_lc) > 2:
                        emit_LC(*pending_lc.pop(0))

            def f_v(v, u=u):
                if u < DC:
                    v.wait_ge(SM["t_mmq"], u + 1)
                    if u >= 2:
                        v.wait_ge(SM["ss_qout%d" % (u % 2)], 16 * (u // 2))
                    v.tensor_copy(vbf(OFF_QOUT[u % 2], TOK), P01[:]).then_inc(SM["v_evq"], 1)
                if u >= 2:
                    h = u - 2
                    if h >= 1:
                        v.wait_ge(SM["t_lf"], h)
                        if h - 1 >= 2:
                            v.wait_ge(SM["ss_lt%d" % ((h - 1) % 2)], 16 * ((h - 1) // 2))
                        v.tensor_copy(ltmp((h - 1) % 2), P01[0:1, 0:1024]).then_inc(SM["v_evl"], 1)
                        v.wait_ge(SM["t_mmpc"], KT * h)
                        if h - 1 >= 2:
                            v.wait_ge(SM["sx_ct%d" % ((h - 1) % 2)], 16 * ((h - 1) // 2))
                        v.tensor_copy(vbf(CTXT[(h - 1) % 2], TOK), P67[:]).then_inc(SM["v_evctx"], 1)
                    for kc in range(GPS_KC0):
                        Kg = KT * h + kc
                        s0 = (kc // 2) * 128
                        v.wait_ge(SM["s_exp"], Kg + 1)
                        acc = vf32(ACC_V[kc % 3], TOK)
                        ex = vbf(OFF_EXPS[Kg % 5], TOK)
                        if kc < 3:
                            if h >= 1:
                                v.wait_ge(SM["v_accbf"], h)
                            ACCSEQ[0] += 1
                            ACCLAST[kc % 3] = ACCSEQ[0]
                            v.tensor_copy(acc[:, s0:1024], ex[:, s0:1024]).then_inc(SM["v_acc"], 1)
                        else:
                            v.wait_ge(SM["v_acc"], ACCLAST[kc % 3])
                            ACCSEQ[0] += 1
                            ACCLAST[kc % 3] = ACCSEQ[0]
                            v.tensor_tensor(acc[:, s0:1024], acc[:, s0:1024],
                                            ex[:, s0:1024], ALU.add).then_inc(SM["v_acc"], 1)
                    a0 = vf32(ACC_V[0], TOK)
                    v.wait_ge(SM["v_acc"], GPS_KC0 * (h + 1))
                    vsync(v, v.tensor_tensor(a0[:, 0:1024], a0[:, 0:1024],
                                             vf32(ACC_V[1], TOK)[:, 0:1024], ALU.add))
                    vsync(v, v.tensor_tensor(a0[:, 128:1024], a0[:, 128:1024],
                                             vf32(ACC_V[2], TOK)[:, 128:1024], ALU.add))
                    v.wait_ge(SM["g_acc"], h + 1)
                    s0g = (GPS_KC0 // 2) * 128
                    vsync(v, v.tensor_tensor(a0[:, s0g:1024], a0[:, s0g:1024],
                                             vf32(ACC_G, TOK)[:, s0g:1024], ALU.add))
                    v.tensor_copy(vbf(ACC_BF, TOK), a0[:]).then_inc(SM["v_accbf"], 1)
            em("v", f_v)

            if u >= 2:
                def f_g(g, u=u):
                    h = u - 2
                    if h >= 1:
                        g.wait_ge(SM["v_accbf"], h)
                    for kc in range(GPS_KC0, KT):
                        Kg = KT * h + kc
                        s0 = (kc // 2) * 128
                        g.wait_ge(SM["s_exp"], Kg + 1)
                        acc = vf32(ACC_G, TOK)
                        ex = vbf(OFF_EXPS[Kg % 5], TOK)
                        if kc == GPS_KC0:
                            gsync(g, g.tensor_copy(acc[:, s0:1024], ex[:, s0:1024]))
                        else:
                            gsync(g, g.tensor_tensor(acc[:, s0:1024], acc[:, s0:1024],
                                                     ex[:, s0:1024], ALU.add))
                    g.nop().then_inc(SM["g_acc"], 1)
                em("g", f_g)
            if u + 2 < DC:
                wq_tgt[u + 2] = emit_wblk_dma(d_wq[u + 2], u + 2, [("t_mmq", u + 1)])

            if u >= 2:
                def f_c(sc, u=u):
                    h = u - 2
                    for kc in range(KT):
                        Kg = KT * h + kc
                        s0 = (kc // 2) * 128
                        sc.wait_ge(SM["t_mms"], Kg + 1)
                        if Kg == 0:
                            sc.wait_ge(SM["v_ini"], 1)
                        if Kg >= 5:
                            sc.wait_ge(SM["t_mmpc"], Kg - 4)
                            h5, kc5 = divmod(Kg - 5, KT)
                            if kc5 < GPS_KC0:
                                sc.wait_ge(SM["v_acc"], GPS_KC0 * h5 + kc5 + 1)
                            else:
                                sc.wait_ge(SM["g_seq"], (KT - GPS_KC0) * h5 + (kc5 - GPS_KC0) + 1)
                        pair = P[1 + kc % 2][:]
                        sc.activation(vbf(OFF_EXPS[Kg % 5], TOK)[:, s0:1024], pair[:, s0:1024],
                                      AF.Exp, bias=sb_consts[:, 0:1], scale=sb_consts[:, 1:2]).then_inc(SM["s_exp"], 1)
                em("c", f_c)

            if u == 2:
                cur_phase[0] = 1
                emit_rb(1, 2)
                cur_phase[0] = 2

            def f_s(sy, u=u):
                if u < DC:
                    sy.wait_ge(SM["v_evq"], u + 1)
                    sy.dma_start(out=d_qsc[u], in_=vbf(OFF_QOUT[u % 2], TOK)).then_inc(SM["ss_qout%d" % (u % 2)], 16)
                    if u >= 2:
                        sy.wait_ge(SM["t_mms"], KT * (u - 1))
                    sy.wait_ge(SM["ss_qout%d" % (u % 2)], 16 * (u // 2 + 1))
                    sy.dma_start(out=vbf(OFF_QH[u % 2], TOK), in_=d_qsc[u]).then_inc(SM["sq_qh%d" % (u % 2)], 16)
                if u >= 3:
                    hm = u - 3
                    sy.wait_ge(SM["v_evl"], hm + 1)
                    sy.dma_start(out=d_linv[hm:hm + 1, :], in_=ltmp(hm % 2)).then_inc(SM["ss_lt%d" % (hm % 2)], 16)
                    sy.wait_ge(SM["v_evctx"], hm + 1)
                    sy.dma_start(out=d_ctxsc[hm], in_=vbf(CTXT[hm % 2], TOK)).then_inc(SM["sx_ct%d" % (hm % 2)], 16)
            em("s", f_s)

        while pending_lc:
            emit_LC(*pending_lc.pop(0))
        emit_lfinal(H - 1)

        def f_v_tail(v):
            v.wait_ge(SM["t_lf"], H)
            v.wait_ge(SM["ss_lt%d" % ((H - 1) % 2)], 16 * ((H - 1) // 2))
            v.tensor_copy(ltmp((H - 1) % 2), P01[0:1, 0:1024]).then_inc(SM["v_evl"], 1)
            v.wait_ge(SM["t_mmpc"], KT * H)
            v.wait_ge(SM["sx_ct%d" % ((H - 1) % 2)], 16 * ((H - 1) // 2))
            v.tensor_copy(vbf(CTXT[(H - 1) % 2], TOK), P67[:]).then_inc(SM["v_evctx"], 1)
        em("v", f_v_tail)

        def f_s_tail(sy):
            hm = H - 1
            sy.wait_ge(SM["v_evl"], hm + 1)
            sy.dma_start(out=d_linv[hm:hm + 1, :], in_=ltmp(hm % 2)).then_inc(SM["ss_lt%d" % (hm % 2)], 16)
            sy.wait_ge(SM["v_evctx"], hm + 1)
            sy.dma_start(out=d_ctxsc[hm], in_=vbf(CTXT[hm % 2], TOK)).then_inc(SM["sx_ct%d" % (hm % 2)], 16)
        em("s", f_s_tail)

        T_MMQ_TOTAL = DC
        T_MMS_TOTAL = KT * H
        T_MMPC_TOTAL = KT * H

        def f_s_linvld(sy):
            sy.wait_ge(SM["ss_lt0"], 16 * (H // 2))
            sy.wait_ge(SM["ss_lt1"], 16 * (H // 2))
            sy.wait_ge(SM["t_mmpc"], T_MMPC_TOTAL)
            sy.dma_start(out=sb_linv32[:, :], in_=d_linv[:]).then_inc(SM["s_linvld"], 16)
        em("s", f_s_linvld)

        def f_v_norm(v):
            v.wait_ge(SM["s_linvld"], 16)
            v.reciprocal(sb_linv32[:, :], sb_linv32[:, :]).then_inc(SM["v_norm"], 1)
        em("v", f_v_norm)

        def f_s_norm(sy):
            sy.wait_ge(SM["v_norm"], 1)
            sy.dma_start(out=d_linv[:], in_=sb_linv32[:, :]).then_inc(SM["s_linv"], 16)
        em("s", f_s_norm)

        def emit_ctxrd(kc):
            slot = kc % 4

            def f(g, kc=kc, slot=slot):
                g.wait_ge(SM["sx_ct%d" % (kc % 2)], 16 * (kc // 2 + 1))
                if kc < 4:
                    g.wait_ge(SM["t_lf"], H)
                    g.wait_ge(SM["v_accbf"], H)
                    g.wait_ge(SM["sx_ct0"], 16 * (H // 2))
                    g.wait_ge(SM["sx_ct1"], 16 * (H // 2))
                else:
                    g.wait_ge(SM["v_nmul"], kc - 3)
                g.dma_start(out=vbf(CTXRD[slot], TOK), in_=d_ctxsc[kc]).then_inc(SM["g_crd%d" % slot], 16)
            em("g", f)

        wo_tgt = [None] * DC
        wo_tgt[0] = emit_wblk_dma(d_wo[0], 0, [("t_mmq", T_MMQ_TOTAL)])
        wo_tgt[1] = emit_wblk_dma(d_wo[1], 1, [("t_mmq", T_MMQ_TOTAL)])
        for kc in range(4):
            emit_ctxrd(kc)
        for hh in range(H):
            slot = hh % 4
            tgt = 16 * (hh // 4 + 1)

            def f_s(sy, hh=hh, slot=slot):
                if hh == 0:
                    sy.wait_ge(SM["s_linv"], 16)
                if hh >= 4:
                    sy.wait_ge(SM["v_nmul"], hh - 3)
                sy.dma_start(out=vf32(OFF_SOFTB[slot], TOK),
                             in_=d_linv[hh:hh + 1, :].broadcast_to([128, TOK])).then_inc(SM["ss_sm%d" % slot], 16)
            em("s", f_s)

            def f_v(v, hh=hh, slot=slot, tgt=tgt):
                v.wait_ge(SM["ss_sm%d" % slot], tgt)
                v.wait_ge(SM["g_crd%d" % slot], tgt)
                if hh == 0:
                    v.wait_ge(SM["t_mmq"], T_MMQ_TOTAL)
                v.tensor_tensor(sb_ctx[:, hh * TOK:(hh + 1) * TOK], vbf(CTXRD[slot], TOK),
                                vf32(OFF_SOFTB[slot], TOK), ALU.mult).then_inc(SM["v_nmul"], 1)
            em("v", f_v)
            if hh + 4 < H:
                emit_ctxrd(hh + 4)

        cur_phase[0] = 4
        r1rdc = {"s": [0, 0, 0, 0], "g": [0, 0, 0, 0]}

        def emit_r1rd(idx, extra_waits, eng="s"):
            slot = idx % 4
            r1rdc[eng][slot] += 1
            tgt = 16 * r1rdc[eng][slot]
            mc = idx % DC
            off = OFF_R1RD[slot]
            pfx = "sr1rd" if eng == "s" else "gr1rd"

            def f(sy, mc=mc, slot=slot, off=off, pfx=pfx, extra_waits=tuple(extra_waits)):
                for sn, val in extra_waits:
                    sy.wait_ge(SM[sn], val)
                sy.dma_start(out=vf32(off, TOK), in_=d_r1[mc]).then_inc(SM[pfx + "%d" % slot], 16)
            em(eng, f)
            return tgt

        r1rd_tgt = [None] * 64
        for mc in range(2, DC):
            wo_tgt[mc] = emit_wblk_dma(d_wo[mc], mc, [("t_mmwo", mc - 1)])

        xm_tgt = [16 * (mc // 2 + 1) for mc in range(DC)]

        def emit_xm(mc):
            slot = mc % 2

            def f_s(sy, mc=mc, slot=slot):
                if mc < 2:
                    sy.wait_ge(SM["v_evq"], DC)
                    sy.wait_ge(SM["ss_qout0"], 16 * (DC // 2))
                    sy.wait_ge(SM["ss_qout1"], 16 * (DC // 2))
                    sy.wait_ge(SM["v_accbf"], H)
                    sy.wait_ge(SM["t_lf"], H)
                else:
                    sy.wait_ge(SM["v_r1"], mc - 1)
                sy.dma_start(out=vf32(OFF_XM32[slot], TOK), in_=d_xm[:, mc, :]).then_inc(SM["sx_xm%d" % slot], 16)
            em("s", f_s)

        emit_xm(0)
        emit_xm(1)

        r1c = [0, 0]
        pending_stats = None
        for mc in range(DC):
            slot = mc % 2

            def f_t(t, mc=mc, slot=slot):
                t.wait_ge(SM["gw%d" % slot], wo_tgt[mc])
                if mc >= 2:
                    t.wait_ge(SM["v_r1"], mc - 1)
                pair = P[mc % 2][:]
                wb = vbf(OFF_WBLK[slot], DC * 128)
                for kc in range(DC):
                    if mc == 0:
                        t.wait_ge(SM["v_nmul"], kc + 1)
                    st = wb[:, kc * 128:(kc + 1) * 128]
                    t.matmul(pair[:, 0:512], st, sb_ctx[:, kc * TOK: kc * TOK + 512],
                             start=(kc == 0), stop=(kc == DC - 1))
                    if kc == DC - 1:
                        t.matmul(pair[:, 512:1024], st, sb_ctx[:, kc * TOK + 512:(kc + 1) * TOK],
                                 start=False, stop=True).then_inc(SM["t_mmwo"], 1)
                    else:
                        t.matmul(pair[:, 512:1024], st, sb_ctx[:, kc * TOK + 512:(kc + 1) * TOK],
                                 start=(kc == 0), stop=False)
            em("t", f_t)
            if pending_stats is not None:
                em("t", pending_stats)

            def f_t_stats(t, mc=mc):
                t.wait_ge(SM["s_stats"], 2 * (mc + 1))
                if mc == 0:
                    t.wait_ge(SM["v_evl"], H)
                    t.wait_ge(SM["v_evctx"], H)
                    t.wait_ge(SM["v_nmul"], H)
                t.matmul(P45[0:1, 0:512], sb_ones[:], sb_r1bf[:, 0:512],
                         start=(mc == 0), stop=(mc == DC - 1))
                t.matmul(P45[0:1, 512:1024], sb_ones[:], sb_r1bf[:, 512:1024],
                         start=(mc == 0), stop=(mc == DC - 1))
                t.matmul(P67[0:1, 0:512], sb_ones[:], sb_sqbf[:, 0:512],
                         start=(mc == 0), stop=(mc == DC - 1))
                t.matmul(P67[0:1, 512:1024], sb_ones[:], sb_sqbf[:, 512:1024],
                         start=(mc == 0), stop=(mc == DC - 1)).then_inc(SM["t_stats"], 1)

            def f_v(v, mc=mc, slot=slot):
                v.wait_ge(SM["t_mmwo"], mc + 1)
                v.wait_ge(SM["sx_xm%d" % slot], xm_tgt[mc])
                if mc == 0:
                    v.wait_ge(SM["s_ini"], S_INI_TOTAL)
                if mc >= 2:
                    v.wait_ge(SM["ss_r1%d" % slot], 16 * (mc // 2))
                    v.wait_ge(SM["s_stats"], 2 * (mc - 1))
                pair = P[mc % 2][:]
                r1b = vf32(OFF_R1BUF[slot], TOK)
                xm32 = vf32(OFF_XM32[slot], TOK)
                v.scalar_tensor_tensor(r1b[:, 0:512], pair[:, 0:512], sb_bo[:, mc:mc + 1],
                                       xm32[:, 0:512], ALU.add, ALU.add)
                v.scalar_tensor_tensor(r1b[:, 512:1024], pair[:, 512:1024], sb_bo[:, mc:mc + 1],
                                       xm32[:, 512:1024], ALU.add, ALU.add).then_inc(SM["v_r1"], 1)
            em("v", f_v)

            def f_c(sc, mc=mc, slot=slot):
                sc.wait_ge(SM["v_r1"], mc + 1)
                if mc >= 1:
                    sc.wait_ge(SM["t_stats"], mc)
                r1b = vf32(OFF_R1BUF[slot], TOK)
                sc.copy(sb_r1bf[:], r1b[:])
                sc.activation(sb_sqbf[:], r1b[:], AF.Square, bias=0.0, scale=1.0).then_inc(SM["s_stats"], 2)
            em("c", f_c)
            pending_stats = f_t_stats

            r1c[slot] += 1

            def f_s2(sy, mc=mc, slot=slot):
                sy.wait_ge(SM["v_r1"], mc + 1)
                sy.dma_start(out=d_r1[mc], in_=vf32(OFF_R1BUF[slot], TOK)).then_inc(SM["ss_r1%d" % slot], 16)
            em("s", f_s2)
            if mc < 4:
                cur_phase[0] = 5
                r1rd_tgt[mc] = emit_r1rd(mc, [("ss_r1%d" % (mc % 2), 16 * (mc // 2 + 1))])
                cur_phase[0] = 4
            if mc + 2 < DC:
                emit_xm(mc + 2)
        em("t", pending_stats)

        def f_v_fin1(v):
            v.wait_ge(SM["t_stats"], DC)
            v.wait_ge(SM["s_linvld"], 16)
            v.wait_ge(SM["v_norm"], 1)
            v.wait_ge(SM["s_linv"], 16)
            vsync(v, v.tensor_copy(sv(0), P45[0:1, 0:1024]))
            vsync(v, v.tensor_copy(sv(1), P67[0:1, 0:1024]))
            vsync(v, v.tensor_scalar_mul(sv(2), sv(0), 1.0 / D))
            vsync(v, v.tensor_scalar_mul(sv(4), sv(1), 1.0 / D))
            vsync(v, v.tensor_mul(sv(5), sv(2), sv(2)))
            vsync(v, v.tensor_sub(sv(4), sv(4), sv(5)))
            v.tensor_scalar_add(sv(4), sv(4), EPS).then_inc(SM["v_stat"], 1)
        em("v", f_v_fin1)

        def f_c_fin1(sc):
            sc.wait_ge(SM["v_stat"], 1)
            sc.activation(sv(3), sv(4), AF.Sqrt, bias=0.0, scale=1.0).then_inc(SM["s_sqrt"], 1)
        em("c", f_c_fin1)

        def f_v_fin1b(v):
            v.wait_ge(SM["s_sqrt"], 1)
            v.reciprocal(sv(3), sv(3)).then_inc(SM["v_stat"], 1)
        em("v", f_v_fin1b)

        def f_s_fin1(sy):
            sy.wait_ge(SM["v_stat"], 2)
            sy.dma_start(out=d_vec4[0:1, :], in_=sv(2)).then_inc(SM["s_vec4"], 16)
            sy.dma_start(out=d_vec4[1:2, :], in_=sv(3)).then_inc(SM["s_vec4"], 16)
            sy.wait_ge(SM["s_vec4"], 32)
            sy.dma_start(out=sb_bcm1[:], in_=d_vec4[0:1, :].broadcast_to([128, TOK])).then_inc(SM["s_bc"], 16)
            sy.dma_start(out=sb_bca1[:], in_=d_vec4[1:2, :].broadcast_to([128, TOK])).then_inc(SM["s_bc"], 16)
        em("s", f_s_fin1)

        cur_phase[0] = 5
        def f_g_lat(g):
            g.wait_ge(SM["t_mmpc"], T_MMPC_TOTAL)
            out = sb_latb.rearrange("p (k q) -> p k q", k=DC)
            g.dma_start(out=out, in_=d_lat[:]).then_inc(SM["s_lat"], 16)
        em("g", f_g_lat)

        def f_s_lbt(sy):
            sy.wait_ge(SM["t_mmpc"], T_MMPC_TOTAL)
            sy.wait_ge(SM["v_nmul"], H)
            sy.dma_start(out=sb_lbt32[:], in_=d_lbt[:]).then_inc(SM["s_lbt"], 16)
        em("s", f_s_lbt)

        def f_v_lbt(v):
            v.wait_ge(SM["s_lbt"], 16)
            v.wait_ge(SM["s_ini"], S_INI_TOTAL)
            v.tensor_scalar(sb_lbtb[:], sb_lbt32[:], sb_lw[:, 0:1], LORA_SCALE,
                            ALU.mult, ALU.mult).then_inc(SM["v_lbt"], 1)
        em("v", f_v_lbt)

        for mc in range(4, DC):
            r1rd_tgt[mc] = emit_r1rd(mc, [("v_ysub", mc - 3),
                                          ("ss_r1%d" % (mc % 2), 16 * (mc // 2 + 1))])
        def yb_sub(v, mc):
            slot = mc % 4
            v.wait_ge(SM["sr1rd%d" % slot], r1rd_tgt[mc])
            if mc == 0:
                v.wait_ge(SM["s_bc"], 32)
            if mc >= 6:
                v.wait_ge(SM["v_y"], mc - 5)
            v.tensor_sub(ytmp(mc)[:], vf32(OFF_R1RD[slot], TOK)[:], sb_bcm1[:]).then_inc(SM["v_ysub"], 1)

        def yb_mul(v, mc):
            v.wait_ge(SM["v_ysub"], mc + 1)
            v.tensor_mul(sb_ybf[:, mc * TOK:(mc + 1) * TOK], ytmp(mc)[:], sb_bca1[:]).then_inc(SM["v_y"], 1)

        def f_v_yb(v):
            for mc in range(DC):
                yb_sub(v, mc)
                if mc >= 3:
                    yb_mul(v, mc - 3)
            for mc in range(DC - 3, DC):
                yb_mul(v, mc)
        em("v", f_v_yb)

        def f_t_h(t):
            t.wait_ge(SM["s_lat"], 16)
            t.wait_ge(SM["v_y"], DC)
            t.wait_ge(SM["v_r1"], DC)
            for kc in range(DC):
                st = sb_latb[:, kc * 128:(kc + 1) * 128]
                t.matmul(P01[:, 0:512], st, sb_ybf[:, kc * TOK: kc * TOK + 512],
                         start=(kc == 0), stop=(kc == DC - 1))
                if kc == DC - 1:
                    t.matmul(P01[:, 512:1024], st, sb_ybf[:, kc * TOK + 512:(kc + 1) * TOK],
                             start=False, stop=True).then_inc(SM["t_mmh"], 1)
                else:
                    t.matmul(P01[:, 512:1024], st, sb_ybf[:, kc * TOK + 512:(kc + 1) * TOK],
                             start=(kc == 0), stop=False)
        em("t", f_t_h)

        def f_v_h(v):
            v.wait_ge(SM["t_mmh"], 1)
            v.tensor_copy(sb_hb[:], P01[:]).then_inc(SM["v_evh"], 1)
        em("v", f_v_h)

        wt_tgt = [None] * DC
        wt_tgt[0] = emit_wblk_dma(d_wt[0], 0, [("t_mmwo", DC), ("t_mmh", 1)])
        wt_tgt[1] = emit_wblk_dma(d_wt[1], 1, [("t_mmwo", DC), ("t_mmh", 1)])

        pending_stats2 = None
        for mc in range(DC):
            slot = mc % 2
            r1rd_tgt[DC + mc] = emit_r1rd(
                DC + mc, [("v_r2", mc - 3)] if mc >= 4 else [("v_ysub", DC)], eng="g")
            if mc + 2 < DC:
                wt_tgt[mc + 2] = emit_wblk_dma(d_wt[mc + 2], mc + 2, [("t_mmf", mc + 1)])

            def f_t(t, mc=mc, slot=slot):
                t.wait_ge(SM["gw%d" % slot], wt_tgt[mc])
                if mc == 0:
                    t.wait_ge(SM["v_evh"], 1)
                    t.wait_ge(SM["v_lbt"], 1)
                if mc >= 2:
                    t.wait_ge(SM["v_r2"], mc - 1)
                pair = P[mc % 2][:]
                wb = vbf(OFF_WBLK[slot], DC * 128)
                for kc in range(DC):
                    st = wb[:, kc * 128:(kc + 1) * 128]
                    t.matmul(pair[:, 0:512], st, sb_ybf[:, kc * TOK: kc * TOK + 512],
                             start=(kc == 0), stop=False)
                    t.matmul(pair[:, 512:1024], st, sb_ybf[:, kc * TOK + 512:(kc + 1) * TOK],
                             start=(kc == 0), stop=False)
                stl = sb_lbtb[:, mc * 128:(mc + 1) * 128]
                t.matmul(pair[:, 0:512], stl, sb_hb[:, 0:512], start=False, stop=True)
                t.matmul(pair[:, 512:1024], stl, sb_hb[:, 512:1024], start=False, stop=True).then_inc(SM["t_mmf"], 1)
            em("t", f_t)
            if pending_stats2 is not None:
                em("t", pending_stats2)

            def f_t_stats(t, mc=mc):
                t.wait_ge(SM["s_st2"], 2 * (mc + 1))
                if mc == 0:
                    t.wait_ge(SM["v_stat"], 2)
                t.matmul(P45[0:1, 0:512], sb_ones[:], sb_r1bf[:, 0:512],
                         start=(mc == 0), stop=(mc == DC - 1))
                t.matmul(P45[0:1, 512:1024], sb_ones[:], sb_r1bf[:, 512:1024],
                         start=(mc == 0), stop=(mc == DC - 1))
                t.matmul(P67[0:1, 0:512], sb_ones[:], sb_sqbf[:, 0:512],
                         start=(mc == 0), stop=(mc == DC - 1))
                t.matmul(P67[0:1, 512:1024], sb_ones[:], sb_sqbf[:, 512:1024],
                         start=(mc == 0), stop=(mc == DC - 1)).then_inc(SM["t_st2"], 1)

            def f_v(v, mc=mc, slot=slot):
                rslot = (DC + mc) % 4
                v.wait_ge(SM["gr1rd%d" % rslot], r1rd_tgt[DC + mc])
                if mc >= 1:
                    v.wait_ge(SM["v_r2"], mc)
                vsync(v, v.tensor_sub(sb_y32[:], vf32(OFF_R1RD[rslot], TOK)[:], sb_bcm1[:]))
                vsync(v, v.tensor_mul(sb_y32[:], sb_y32[:], sb_bca1[:]))
                v.wait_ge(SM["t_mmf"], mc + 1)
                if mc < 2:
                    v.wait_ge(SM["v_r1"], DC)
                if mc >= 2:
                    v.wait_ge(SM["ss_r2%d" % slot], 16 * (mc // 2))
                    v.wait_ge(SM["s_st2"], 2 * (mc - 1))
                pair = P[mc % 2][:]
                r2b = vf32(OFF_R2BUF[slot], TOK)
                v.scalar_tensor_tensor(r2b[:, 0:512], pair[:, 0:512], sb_bi[:, mc:mc + 1],
                                       sb_y32[:, 0:512], ALU.add, ALU.add)
                v.scalar_tensor_tensor(r2b[:, 512:1024], pair[:, 512:1024], sb_bi[:, mc:mc + 1],
                                       sb_y32[:, 512:1024], ALU.add, ALU.add).then_inc(SM["v_r2"], 1)
            em("v", f_v)

            def f_c(sc, mc=mc, slot=slot):
                sc.wait_ge(SM["v_r2"], mc + 1)
                if mc >= 1:
                    sc.wait_ge(SM["t_st2"], mc)
                if mc == 0:
                    sc.wait_ge(SM["t_stats"], DC)
                r2b = vf32(OFF_R2BUF[slot], TOK)
                sc.copy(sb_r1bf[:], r2b[:])
                sc.activation(sb_sqbf[:], r2b[:], AF.Square, bias=0.0, scale=1.0).then_inc(SM["s_st2"], 2)
            em("c", f_c)
            pending_stats2 = f_t_stats

            def f_s2(sy, mc=mc, slot=slot):
                sy.wait_ge(SM["v_r2"], mc + 1)
                sy.dma_start(out=d_r2[mc], in_=vf32(OFF_R2BUF[slot], TOK)).then_inc(SM["ss_r2%d" % slot], 16)
            em("s", f_s2)
        em("t", pending_stats2)

        def f_v_fin2(v):
            v.wait_ge(SM["t_st2"], DC)
            vsync(v, v.tensor_copy(sv(0), P45[0:1, 0:1024]))
            vsync(v, v.tensor_copy(sv(1), P67[0:1, 0:1024]))
            vsync(v, v.tensor_scalar_mul(sv(2), sv(0), 1.0 / D))
            vsync(v, v.tensor_scalar_mul(sv(4), sv(1), 1.0 / D))
            vsync(v, v.tensor_mul(sv(5), sv(2), sv(2)))
            vsync(v, v.tensor_sub(sv(4), sv(4), sv(5)))
            v.tensor_scalar_add(sv(4), sv(4), EPS).then_inc(SM["v_stat2"], 1)
        em("v", f_v_fin2)

        def f_c_fin2(sc):
            sc.wait_ge(SM["v_stat2"], 1)
            sc.activation(sv(3), sv(4), AF.Sqrt, bias=0.0, scale=1.0).then_inc(SM["s_sqrt2"], 1)
        em("c", f_c_fin2)

        def f_v_fin2b(v):
            v.wait_ge(SM["s_sqrt2"], 1)
            v.reciprocal(sv(3), sv(3)).then_inc(SM["v_stat2"], 1)
        em("v", f_v_fin2b)

        def f_s_fin2(sy):
            sy.wait_ge(SM["v_stat2"], 2)
            sy.dma_start(out=d_vec4[2:3, :], in_=sv(2)).then_inc(SM["s_vec4"], 16)
            sy.dma_start(out=d_vec4[3:4, :], in_=sv(3)).then_inc(SM["s_vec4"], 16)
            sy.wait_ge(SM["s_vec4"], 64)
            sy.dma_start(out=sb_bcm2[:], in_=d_vec4[2:3, :].broadcast_to([128, TOK])).then_inc(SM["s_bc"], 16)
            sy.dma_start(out=sb_bca2[:], in_=d_vec4[3:4, :].broadcast_to([128, TOK])).then_inc(SM["s_bc"], 16)
        em("s", f_s_fin2)

        cur_phase[0] = 6
        r2rd_tgt6 = [16 * (mc // 4 + 1) for mc in range(DC)]

        def emit_r2rd(mc):
            slot = mc % 4

            def f_s(sy, mc=mc, slot=slot):
                sy.wait_ge(SM["ss_r2%d" % (mc % 2)], 16 * (mc // 2 + 1))
                if slot >= 2 and mc < 4:
                    sy.wait_ge(SM["t_mmf"], DC)
                if mc >= 4:
                    sy.wait_ge(SM["v_osub"], mc - 3)
                sy.dma_start(out=vf32(OFF_R2RD[slot], TOK), in_=d_r2[mc]).then_inc(SM["sr2rd%d" % slot], 16)
            em("g", f_s)

        for mc in range(4):
            emit_r2rd(mc)

        def o_sub(v, mc):
            slot = mc % 4
            v.wait_ge(SM["sr2rd%d" % slot], r2rd_tgt6[mc])
            if mc == 0:
                v.wait_ge(SM["s_bc"], 64)
            if mc >= 6:
                v.wait_ge(SM["v_out"], mc - 5)
            v.tensor_sub(ytmp(mc)[:], vf32(OFF_R2RD[slot], TOK)[:], sb_bcm2[:]).then_inc(SM["v_osub"], 1)

        def o_mul(v, mc):
            v.wait_ge(SM["v_osub"], mc + 1)
            if mc >= 2:
                v.wait_ge(SM["so_out%d" % (mc % 2)], 16 * (mc // 2))
            v.tensor_mul(vf32(OFF_OUTB[mc % 2], TOK)[:], ytmp(mc)[:], sb_bca2[:]).then_inc(SM["v_out"], 1)

        def f_v6(v):
            for mc in range(DC):
                o_sub(v, mc)
                if mc >= 3:
                    o_mul(v, mc - 3)
            for mc in range(DC - 3, DC):
                o_mul(v, mc)
        em("v", f_v6)

        for mc in range(DC):

            def f_s2(sy, mc=mc):
                sy.wait_ge(SM["v_out"], mc + 1)
                sy.dma_start(out=d_out[mc], in_=vf32(OFF_OUTB[mc % 2], TOK)).then_inc(SM["so_out%d" % (mc % 2)], 16)
            em("s", f_s2)
            if mc + 4 < DC:
                emit_r2rd(mc + 4)

        def f_s_final(sy):
            sy.wait_ge(SM["so_out0"], 16 * (DC // 2))
            sy.wait_ge(SM["so_out1"], 16 * (DC // 2))
        em("s", f_s_final)

        dma_table = [
            (0, "s_ini", 48),
            (1, "gx0", 128), (1, "gx1", 128), (1, "s_wkv", 16), (1, "s_xmbf", 64),
            (1, "sc_st0", 128), (1, "sc_st1", 128), (1, "s_kvrb", 192), (1, "s_kvrb2", 192),
            (2, "gw0", 256), (2, "gw1", 256),
            (2, "ss_qout0", 256), (2, "ss_qout1", 256),
            (2, "s_masks", 32), (2, "sq_qh0", 256), (2, "sq_qh1", 256),
            (2, "ss_lt0", 256), (2, "ss_lt1", 256), (2, "s_linvld", 16),
            (2, "s_linv", 16), (2, "ss_sm0", 128), (2, "ss_sm1", 128),
            (2, "ss_sm2", 128), (2, "ss_sm3", 128),
            (2, "sx_ct0", 256), (2, "sx_ct1", 256),
            (2, "g_crd0", 128), (2, "g_crd1", 128), (2, "g_crd2", 128), (2, "g_crd3", 128),
            (4, "gw0", 256), (4, "gw1", 256),
            (4, "sx_xm0", 256), (4, "sx_xm1", 256),
            (4, "ss_r10", 256), (4, "ss_r11", 256),
            (4, "s_vec4", 32), (4, "s_bc", 32),
            (5, "gw0", 256), (5, "gw1", 256),
            (5, "s_lat", 16), (5, "s_lbt", 16),
            (5, "sr1rd0", 128), (5, "sr1rd1", 128),
            (5, "sr1rd2", 128), (5, "sr1rd3", 128),
            (5, "gr1rd0", 128), (5, "gr1rd1", 128),
            (5, "gr1rd2", 128), (5, "gr1rd3", 128),
            (5, "ss_r20", 256), (5, "ss_r21", 256),
            (5, "s_vec4", 32), (5, "s_bc", 32),
            (6, "sr2rd0", 128), (6, "sr2rd1", 128),
            (6, "sr2rd2", 128), (6, "sr2rd3", 128),
            (6, "so_out0", 256), (6, "so_out1", 256),
        ]
        fin = {}
        for ph, sn, inc in dma_table:
            if ph <= PHASES:
                fin[sn] = fin.get(sn, 0) + inc

        def f_dma_fin(sy):
            for sn in sorted(fin):
                sy.wait_ge(SM[sn], fin[sn])
        prog["s"].append((0, f_dma_fin))

        with nc.Block() as block:
            @block.gpsimd
            def _(g):
                for ph, f in prog["g"]:
                    if ph <= PHASES:
                        f(g)

            @block.sync
            def _(sy):
                for ph, f in prog["s"]:
                    if ph <= PHASES:
                        f(sy)

            @block.tensor
            def _(t):
                for ph, f in prog["t"]:
                    if ph <= PHASES:
                        f(t)

            @block.vector
            def _(v):
                for ph, f in prog["v"]:
                    if ph <= PHASES:
                        f(v)

            @block.scalar
            def _(sc):
                for ph, f in prog["c"]:
                    if ph <= PHASES:
                        f(sc)

    return nc


def _stage_inputs(L0_attn_x, L0_attn_Wq, L0_attn_Wkv, L0_attn_Wo, L0_attn_bo,
                  L0_ffn_lora_A, L0_ffn_lora_B, L0_ffn_W, L0_ffn_bias,
                  L0_ffn_lora_weights):
    f32 = np.float32
    x = np.asarray(L0_attn_x, f32)
    Wq = np.asarray(L0_attn_Wq, f32)
    Wkv = np.asarray(L0_attn_Wkv, f32)
    Wo = np.asarray(L0_attn_Wo, f32)
    bo = np.asarray(L0_attn_bo, f32)
    lA = np.asarray(L0_ffn_lora_A, f32)
    lB = np.asarray(L0_ffn_lora_B, f32)
    W = np.asarray(L0_ffn_W, f32)
    bi = np.asarray(L0_ffn_bias, f32)
    lw = np.asarray(L0_ffn_lora_weights, f32)

    xT = x.reshape(TOKALL, D).T
    xs_all = np.ascontiguousarray(
        xT.reshape(DC, 128, 2 * NXT, XTW).transpose(2, 1, 0, 3))
    wq = np.ascontiguousarray(
        Wq.reshape(DC, 128, DC, 128).transpose(2, 1, 0, 3))
    wo = np.ascontiguousarray(
        Wo.reshape(DC, 128, DC, 128).transpose(2, 1, 0, 3))
    wt = np.ascontiguousarray(
        W.reshape(DC, 128, DC, 128).transpose(0, 3, 2, 1))
    lat = np.ascontiguousarray(
        lA.reshape(128, D).T.reshape(DC, 128, 128).transpose(1, 0, 2))
    lbt = np.ascontiguousarray(lB.transpose(0, 2, 1).reshape(128, D))
    lwr = np.ascontiguousarray(np.repeat(lw, 16).reshape(128, 1))
    bo_c = np.ascontiguousarray(bo.reshape(DC, 128).T)
    bi_c = np.ascontiguousarray(bi.reshape(DC, 128).T)

    wkv_by_bq = []
    for bq in range(B):
        sl = Wkv[:, 512 * bq:512 * (bq + 1)]
        perm = np.concatenate([np.arange(0, 128), np.arange(256, 384),
                               np.arange(128, 256), np.arange(384, 512)])
        slr = sl[:, perm]
        wkv_by_bq.append(np.ascontiguousarray(
            slr.reshape(DC, 128, 512).transpose(1, 0, 2)))

    tri = np.where(np.arange(128)[:, None] <= np.arange(128)[None, :],
                   np.float32(0.0), np.float32(MASK_NEG))
    dead = np.full((128, 128), MASK_NEG, f32)
    full = np.zeros((128, 128), f32)

    in_maps = []
    rows_by_core = []
    for c in range(8):
        bq = c // 2
        odd = c % 2
        rows = np.concatenate([np.arange(128) + 128 * (2 * s + odd) for s in range(QTS)])
        rows_by_core.append(rows)
        xm = np.ascontiguousarray(
            x[bq][rows, :].T.reshape(DC, 128, TOK).transpose(1, 0, 2))
        mk = np.empty((KT, 128, 128), f32)
        for kc in range(KT):
            s = kc // 2
            gidx = 2 * s + odd
            if kc < gidx:
                mk[kc] = full
            elif kc == gidx:
                mk[kc] = tri
            else:
                mk[kc] = dead
        masks = np.ascontiguousarray(mk.transpose(2, 0, 1))
        xs = xs_all[odd * NXT:(odd + 1) * NXT]
        in_maps.append({
            "xs": xs, "xm": xm, "wq": wq, "wo": wo, "wt": wt,
            "wkv": wkv_by_bq[bq], "lat": lat, "lbt": lbt, "lw": lwr,
            "bo": bo_c, "bi": bi_c, "masks": masks,
            "iden": np.eye(128, dtype=f32),
        })
    return in_maps, rows_by_core


def kernel(**inputs):
    in_maps, rows_by_core = _stage_inputs(**inputs)
    if "nc" not in _CACHED:
        _CACHED["nc"] = build_nc()
    nc = _CACHED["nc"]
    res = run_bass_kernel_spmd(nc, in_maps, core_ids=list(range(8)))
    out = np.empty((B, T, D), np.float32)
    for c in range(8):
        y_t = np.asarray(res.results[c]["out"]).reshape(D, TOK)
        out[c // 2, rows_by_core[c], :] = y_t.T
    return out
